# version 07 (final)
# baseline (speedup 1.0000x reference)
import numpy as np
import ml_dtypes
import concourse.bass as bass
import concourse.mybir as mybir
from concourse.bass_utils import run_bass_kernel_spmd

F32 = mybir.dt.float32
BF16 = mybir.dt.bfloat16
AX = mybir.AxisListType.X
ALU = mybir.AluOpType
AF = mybir.ActivationFunctionType

NB = 2
S = 4096
T = NB * S
D = 1024
FF = 4096
NE = 8
NH = 4
DH = 256
ND = D // 128
NF = FF // 128
TCH = 384
CHUNKS = [(s0, min(TCH, T - s0)) for s0 in range(0, T, TCH)]
NCH = len(CHUNKS)
NTT = T // 128
ENGS = ("sync", "gpsimd", "tensor", "vector", "scalar")
DEBUG_OUTPUTS = False


class Sem:
    def __init__(self, nc, name):
        self.cm = nc.semaphore(name)
        self.h = self.cm.__enter__()
        self.v = 0


class Sch:

    def __init__(self, nc):
        self.nc = nc
        self.phase = "setup"
        self.q = {e: [] for e in ENGS}
        self.wm = {e: {} for e in ENGS}
        self.pend = {e: [] for e in ENGS}
        self.dS = Sem(nc, "dS")
        self.dG = Sem(nc, "dG")
        self.dV = Sem(nc, "dV")
        self.dC = Sem(nc, "dC")
        self.tT = Sem(nc, "tT")
        self.vV = Sem(nc, "vV")
        self.cS = Sem(nc, "cS")
        self.gC = Sem(nc, "gC")
        self.sems = [self.dS, self.dG, self.dV, self.dC, self.tT, self.vV, self.cS, self.gC]
        self.bank_free = {i: None for i in range(8)}
        self.ln_last = [None, None]
        self.ln_stat = [None, None]

    def emit(self, eng, fn, waits=(), inc=None, by=1):
        ws = []
        for w in list(self.pend[eng]) + list(waits):
            if w is None:
                continue
            s, v = w
            if v > self.wm[eng].get(id(s), 0):
                ws.append((s, v))
                self.wm[eng][id(s)] = v
        self.pend[eng] = []
        val = None
        if inc is not None:
            inc.v += by
            val = inc.v
        self.q[eng].append((ws, fn, inc, by, self.phase))
        return val

    def barrier(self):
        snap = [(s, s.v) for s in self.sems]
        for e in ENGS:
            self.pend[e] = list(snap)

    def run(self, eng, eobj):
        cur = None
        sid = None
        for ws, fn, inc, by, ph in self.q[eng]:
            if ph != cur:
                if cur is not None:
                    self.nc.leave_named_scope(cur, sid, False)
                sid, _ = self.nc.enter_named_scope(ph, False)
                cur = ph
            for s, v in ws:
                eobj.wait_ge(s.h, v)
            ins = fn(eobj)
            if inc is not None:
                ins.then_inc(inc.h, by)
        if cur is not None:
            self.nc.leave_named_scope(cur, sid, False)


def build_core_program():
    nc = bass.Bass("TRN2", target_bir_lowering=False)
    x_in = nc.dram_tensor("x", [T, D], F32, kind="ExternalInput").ap()
    w1_in = nc.dram_tensor("w1", [NE, D, FF], F32, kind="ExternalInput").ap()
    w2_in = nc.dram_tensor("w2", [NE, FF, D], F32, kind="ExternalInput").ap()
    gw_in = nc.dram_tensor("gw", [D, NE], F32, kind="ExternalInput").ap()
    wqkv_in = nc.dram_tensor("wqkv", [24, 128, 128], F32, kind="ExternalInput").ap()
    wi_in = nc.dram_tensor("wi", [NH, 3 * D], F32, kind="ExternalInput").ap()
    wf_in = nc.dram_tensor("wf", [NH, 3 * D], F32, kind="ExternalInput").ap()
    bi_in = nc.dram_tensor("bi", [NH, 1], F32, kind="ExternalInput").ap()
    bf_in = nc.dram_tensor("bfg", [NH, 1], F32, kind="ExternalInput").ap()
    lnw_in = nc.dram_tensor("lnw", [1, D], F32, kind="ExternalInput").ap()
    msk_in = nc.dram_tensor("msk", [4, 128, 512], F32, kind="ExternalInput").ap()
    idn_in = nc.dram_tensor("idn", [128, 128], F32, kind="ExternalInput").ap()
    out_hb = nc.dram_tensor("out", [T, D], F32, kind="ExternalOutput").ap()
    K_ = "ExternalOutput" if DEBUG_OUTPUTS else "Internal"
    XN = nc.dram_tensor("XN", [T, D], BF16, kind=K_).ap()
    X2 = nc.dram_tensor("X2", [T, D], F32, kind=K_).ap()
    X3 = nc.dram_tensor("X3", [T, D], F32, kind=K_).ap()
    ATT = nc.dram_tensor("ATT", [T, D], F32, kind=K_).ap()
    YS = nc.dram_tensor("YS", [T, D], F32, kind=K_).ap()
    GS = nc.dram_tensor("GS", [NB, 3, NH, S], F32, kind=K_).ap()

    off = [17408]

    def P(name, shape, dt):
        h = nc.alloc_sbuf_tensor_at(name, shape, dt, offset=off[0])
        nbytes = int(np.prod(shape[1:])) * (2 if dt == BF16 else 4)
        off[0] += ((nbytes + 31) // 32) * 32
        return h

    gwb = P("gwb", [128, ND * NE], BF16)
    wqkvb = P("wqkvb", [128, 24 * 128], BF16)
    s3aT = P("s3aT", [128, ND * NH], BF16)
    s3fT = P("s3fT", [128, ND * NH], BF16)
    bi_sb = P("bi_sb", [NH, 1], F32)
    bf_sb = P("bf_sb", [NH, 1], F32)
    ones1 = P("ones1", [1, 128], F32)
    comb = P("comb", [128, NTT * NE], F32)
    grow = P("grow", [1, 256], F32)
    epsc = P("epsc", [128, 1], F32)
    idn_sb = P("idn_sb", [128, 128], F32)
    PERS = ((off[0] + 31) // 32) * 32

    F0 = PERS
    R0 = F0 + 161792
    L0 = R0 + 10368

    def A(name, shape, dt, at):
        return nc.alloc_sbuf_tensor_at(name, shape, dt, offset=F0 + at)

    w1sb = A("w1sb", [128, ND * FF], BF16, 0)
    w2sb = A("w2sb", [128, NF * D], BF16, 65536)
    xnTb = [A(f"xnT{i}", [128, ND * TCH], BF16, 131072 + i * 6144) for i in range(2)]
    hTr = A("hTr", [128, 8 * TCH], BF16, 143360)
    ysb = A("ysb", [128, 3 * D], F32, 149504)
    xg_sb = A("xg", [128, ND * 512], BF16, 161792)
    glT = A("glT", [NE, 512], F32, 169984)
    gl_t = A("gl_t", [128, NE], F32, 172032)
    r8a = A("r8a", [128, NE], F32, 172064)
    r8b = A("r8b", [128, NE], F32, 172096)
    rc = A("rc", [128, 8], F32, 172128)
    xt2 = [A(f"xt{i}", [128, D], F32, 172160 + i * 4096) for i in range(2)]
    yt2 = [A(f"yt{i}", [128, D], F32, 180352 + i * 4096) for i in range(2)]
    xnb2 = [A(f"xnb{i}", [128, D], BF16, 188544 + i * 2048) for i in range(2)]
    sqsc = A("sqsc", [128, D], F32, 192640)
    stat = A("stat", [128, 16], F32, 196736)
    xhT = A("xhT", [128, 2 * S], BF16, 0)
    qT = A("qT", [128, 2 * S], BF16, 16384)
    kT = A("kT", [128, 2 * S], BF16, 32768)
    zer4 = A("zer4", [NH, S], F32, 49152)
    v_sb = A("v_sb", [128, 32 * 264], BF16, 49152)
    mb_sb = A("mb_sb", [128, 5 * 512], F32, 66048)
    D_sb = [A(f"Dsb{i}", [128, 512], F32, 76288 + i * 2048) for i in range(2)]
    cT_sb = [A(f"cT{i}", [128, 512], BF16, 80384 + i * 1024) for i in range(2)]
    cols = A("cols", [128, 64], F32, 82432)
    hout = [A(f"hout{i}", [128, DH], F32, 82688 + i * 1024) for i in range(2)]
    no_sb = [A(f"no_sb{i}", [128, DH], F32, 84736 + i * 1024) for i in range(4)]
    sq2 = A("sq2", [128, DH], F32, 88832)
    ast = A("ast", [128, 64], F32, 89856)
    msk_sb = A("msk_sb", [128, 4 * 512], F32, 90112)
    lnw_sb = A("lnw_sb", [1, D], F32, 98304)
    gamma = A("gamma", [128, NH * DH], F32, 102400)
    ig4 = A("ig4", [NH, S], F32, 106496)
    fg4 = A("fg4", [NH, S], F32, 122880)
    t4a = A("t4a", [NH, S], F32, 139264)
    t4b = A("t4b", [NH, S], F32, 155648)
    arow = A("arow", [1, S], F32, 106496)
    mrow = A("mrow", [1, S], F32, 122880)
    erow = A("erow", [1, S], F32, 139264)
    s3w = A("s3w", [NH, 3 * D], F32, 0)
    s3s = A("s3s", [NH, D], F32, 12288)

    PS = [nc.alloc_psum_tensor(f"ps{i}", [128, 512], F32) for i in range(8)]

    sch = Sch(nc)
    dS, dG, dV, dC = sch.dS, sch.dG, sch.dV, sch.dC
    tT, vV, cS, gC = sch.tT, sch.vV, sch.cS, sch.gC
    bank_wait = lambda i: sch.bank_free[i]

    def bank_set(i, sem, val):
        sch.bank_free[i] = (sem, val)

    def setup():
        v_idn = sch.emit("sync", lambda e: e.dma_start(out=idn_sb[:, :], in_=idn_in), inc=dS, by=16)
        v_bi = sch.emit("sync", lambda e: e.dma_start(out=bi_sb[:, :], in_=bi_in), inc=dS, by=16)
        v_bf = sch.emit("sync", lambda e: e.dma_start(out=bf_sb[:, :], in_=bf_in), inc=dS, by=16)
        v_gw = sch.emit("gpsimd", lambda e: e.dma_start(
            out=gwb[:, :].rearrange("p (j e) -> p j e", j=ND),
            in_=gw_in.rearrange("(j p) e -> p j e", p=128)), inc=dG, by=16)
        v_wqkv = sch.emit("gpsimd", lambda e: e.dma_start(
            out=wqkvb[:, :].rearrange("p (m c) -> p m c", m=24),
            in_=wqkv_in.rearrange("m p c -> p m c")), inc=dG, by=16)
        v_o1 = sch.emit("vector", lambda e: e.memset(ones1[0:1, :], 1.0), inc=vV)
        v_eps = sch.emit("vector", lambda e: e.memset(epsc[:, :], 1e-5), inc=vV)

        prev_tv = None
        prev_a2 = None
        for which in range(2):
            src = wi_in if which == 0 else wf_in
            dstT = s3aT if which == 0 else s3fT
            lw = []
            if which == 1:
                lw = [(vV, prev_a2), (tT, prev_tv)]
            wv = sch.emit("sync", lambda e, src=src: e.dma_start(out=s3w[0:NH, :], in_=src),
                          waits=lw, inc=dS, by=16)
            a1 = sch.emit("vector", lambda e: e.tensor_add(s3s[0:NH, :], s3w[0:NH, 0:D], s3w[0:NH, D:2 * D]),
                          waits=[(dS, wv)], inc=vV)
            a2 = sch.emit("vector", lambda e: e.tensor_add(s3s[0:NH, :], s3s[0:NH, :], s3w[0:NH, 2 * D:3 * D]),
                          waits=[(vV, a1)], inc=vV)
            prev_a2 = a2
            for j in range(ND):
                tv = sch.emit("tensor", lambda e, j=j: e.transpose(
                    PS[0][0:128, 0:NH], s3s[0:NH, j * 128:(j + 1) * 128], idn_sb[0:NH, 0:NH]),
                    waits=[(vV, a2), (dS, v_idn), bank_wait(0)], inc=tT)
                prev_tv = tv
                ev = sch.emit("vector", lambda e, j=j, d=dstT: e.tensor_copy(
                    d[:, j * NH:(j + 1) * NH], PS[0][0:128, 0:NH]),
                    waits=[(tT, tv)], inc=vV)
                bank_set(0, vV, ev)
        return dict(v_gw=v_gw, v_wqkv=v_wqkv, v_bi=v_bi, v_bf=v_bf, v_eps=v_eps,
                    v_idn=v_idn, v_o1=v_o1, t_setup=sch.tT.v, v_setup=sch.vV.v)

    def ln_pass(src0, src1, out_x, out_xn, xn_vals, x_vals, i0, i1,
                src0_vals=None, src1_vals=None):
        for i in range(i0, i1):
            p = i % 2
            so8 = p * 8
            r0 = i * 128
            lw = [sch.ln_last[p]] if sch.ln_last[p] else []
            if src0_vals is not None:
                lw.append(src0_vals[i])
            vx = sch.emit("sync", lambda e, o=xt2[p], s=src0, r=r0: e.dma_start(
                out=o[:, :], in_=s[r:r + 128, :]), waits=lw, inc=dS, by=16)
            statw = [(vV, sch.ln_stat[p])] if sch.ln_stat[p] else []
            if src1 is not None:
                lw2 = [src1_vals[i]] if src1_vals is not None else []
                vy = sch.emit("sync", lambda e, o=yt2[p], s=src1, r=r0: e.dma_start(
                    out=o[:, :], in_=s[r:r + 128, :]), waits=lw2, inc=dS, by=16)
                vadd = sch.emit("vector", lambda e, p=p, so8=so8: e.scalar_tensor_tensor(
                    out=xt2[p][:, :], in0=xt2[p][:, :], scalar=1.0, in1=yt2[p][:, :],
                    op0=ALU.mult, op1=ALU.add, accum_out=stat[:, so8:so8 + 1]),
                    waits=[(dS, vx), (dS, vy)] + statw, inc=vV)
                xready = (vV, vadd)
            else:
                vsum = sch.emit("vector", lambda e, p=p, so8=so8: e.tensor_scalar(
                    out=sqsc[:, :], in0=xt2[p][:, :], scalar1=1.0, scalar2=0.0,
                    op0=ALU.mult, op1=ALU.add, accum_out=stat[:, so8:so8 + 1]),
                    waits=[(dS, vx)] + statw, inc=vV)
                xready = (vV, vsum)
            consumers = []
            if out_x is not None:
                so = sch.emit("sync", lambda e, p=p, r=r0, o=out_x: e.dma_start(
                    out=o[r:r + 128, :], in_=xt2[p][:, :]), waits=[xready], inc=dS, by=16)
                consumers.append((dS, so))
                x_vals[i] = (dS, so)
            if out_xn is not None:
                vsq = sch.emit("scalar", lambda e, p=p, so8=so8: e.activation(
                    out=sqsc[:, :], in_=xt2[p][:, :], func=AF.Square,
                    accum_out=stat[:, so8 + 1:so8 + 2]),
                    waits=[xready], inc=cS)
                m1 = sch.emit("vector", lambda e, so8=so8: e.tensor_scalar(
                    out=stat[:, so8 + 2:so8 + 3], in0=stat[:, so8:so8 + 1],
                    scalar1=1.0 / D, scalar2=None, op0=ALU.mult),
                    waits=[xready], inc=vV)
                m2 = sch.emit("vector", lambda e, so8=so8: e.tensor_scalar(
                    out=stat[:, so8 + 3:so8 + 4], in0=stat[:, so8 + 1:so8 + 2],
                    scalar1=1.0 / D, scalar2=None, op0=ALU.mult),
                    waits=[(cS, vsq)], inc=vV)
                m3 = sch.emit("vector", lambda e, so8=so8: e.tensor_mul(
                    stat[:, so8 + 4:so8 + 5], stat[:, so8 + 2:so8 + 3], stat[:, so8 + 2:so8 + 3]),
                    waits=[(vV, m1)], inc=vV)
                m4 = sch.emit("vector", lambda e, so8=so8: e.tensor_sub(
                    stat[:, so8 + 5:so8 + 6], stat[:, so8 + 3:so8 + 4], stat[:, so8 + 4:so8 + 5]),
                    waits=[(vV, m2), (vV, m3)], inc=vV)
                l1 = sch.emit("scalar", lambda e, so8=so8: e.activation(
                    out=stat[:, so8 + 6:so8 + 7], in_=stat[:, so8 + 5:so8 + 6],
                    func=AF.Ln, bias=epsc[:, 0:1]), waits=[(vV, m4)], inc=cS)
                l2 = sch.emit("scalar", lambda e, so8=so8: e.activation(
                    out=stat[:, so8 + 7:so8 + 8], in_=stat[:, so8 + 6:so8 + 7],
                    func=AF.Exp, scale=-0.5), waits=[(cS, l1)], inc=cS)
                vn = sch.emit("vector", lambda e, p=p, so8=so8: e.tensor_scalar(
                    out=xnb2[p][:, :], in0=xt2[p][:, :], scalar1=stat[:, so8 + 2:so8 + 3],
                    scalar2=stat[:, so8 + 7:so8 + 8], op0=ALU.subtract, op1=ALU.mult),
                    waits=[(cS, l2)], inc=vV)
                sn = sch.emit("sync", lambda e, p=p, r=r0, o=out_xn: e.dma_start(
                    out=o[r:r + 128, :], in_=xnb2[p][:, :]), waits=[(vV, vn)], inc=dS, by=16)
                consumers.append((dS, sn))
                consumers.append((vV, vn))
                sch.ln_stat[p] = vn
                xn_vals[i] = (dS, sn)
            else:
                sch.ln_stat[p] = xready[1]
            if not consumers:
                consumers.append(xready)
            sch.ln_last[p] = consumers[0]
            for c in consumers[1:]:
                sch.pend["sync"].append(c)

    def router(sv, xn_vals, comb_vals, g0, g1, prev_mm, qeng, qsem):
        for g in range(g0, g1):
            lw = [(tT, prev_mm)] if prev_mm else []
            lw.append(xn_vals[g * 4 + 3])
            xv = None
            for j in range(ND):
                xv = sch.emit(qeng, lambda e, g=g, j=j: e.dma_start_transpose(
                    out=xg_sb[:, j * 512:(j + 1) * 512],
                    in_=XN[g * 512:(g + 1) * 512, j * 128:(j + 1) * 128]),
                    waits=(lw if j == 0 else []), inc=qsem, by=16)
            mmv = None
            for j in range(ND):
                mmv = sch.emit("tensor", lambda e, j=j: e.matmul(
                    PS[0][0:NE, :], gwb[:, j * NE:(j + 1) * NE], xg_sb[:, j * 512:(j + 1) * 512],
                    start=(j == 0), stop=(j == ND - 1)),
                    waits=([(qsem, xv), (dG, sv["v_gw"]), bank_wait(0)] if j == 0 else []),
                    inc=(tT if j == ND - 1 else None))
            prev_mm = mmv
            ev = sch.emit("vector", lambda e: e.tensor_copy(glT[0:NE, :], PS[0][0:NE, :]),
                          waits=[(tT, mmv)], inc=vV)
            bank_set(0, vV, ev)
            for q in range(4):
                i = g * 4 + q
                tv = sch.emit("tensor", lambda e, q=q: e.transpose(
                    PS[1][0:128, 0:NE], glT[0:NE, q * 128:(q + 1) * 128], idn_sb[0:NE, 0:NE]),
                    waits=[(vV, ev), bank_wait(1)], inc=tT)
                e0 = sch.emit("vector", lambda e: e.tensor_copy(gl_t[:, :], PS[1][0:128, 0:NE]),
                              waits=[(tT, tv)], inc=vV)
                bank_set(1, vV, e0)
                e1 = sch.emit("vector", lambda e: e.reduce_max(rc[:, 0:1], gl_t[:, :], axis=AX),
                              waits=[(vV, e0)], inc=vV)
                e2 = sch.emit("vector", lambda e: e.tensor_scalar(
                    out=r8a[:, :], in0=gl_t[:, :], scalar1=rc[:, 0:1], scalar2=None,
                    op0=ALU.is_equal), waits=[(vV, e1)], inc=vV)
                e3 = sch.emit("vector", lambda e: e.scalar_tensor_tensor(
                    out=r8b[:, :], in0=r8a[:, :], scalar=-1e30, in1=gl_t[:, :],
                    op0=ALU.mult, op1=ALU.add), waits=[(vV, e2)], inc=vV)
                e4 = sch.emit("vector", lambda e: e.reduce_max(rc[:, 1:2], r8b[:, :], axis=AX),
                              waits=[(vV, e3)], inc=vV)
                e5 = sch.emit("vector", lambda e: e.tensor_scalar(
                    out=r8b[:, :], in0=r8b[:, :], scalar1=rc[:, 1:2], scalar2=None,
                    op0=ALU.is_equal), waits=[(vV, e4)], inc=vV)
                e6 = sch.emit("vector", lambda e: e.tensor_sub(rc[:, 2:3], rc[:, 1:2], rc[:, 0:1]),
                              waits=[(vV, e5)], inc=vV)
                e7 = sch.emit("scalar", lambda e: e.activation(
                    out=rc[:, 3:4], in_=rc[:, 2:3], func=AF.Exp), waits=[(vV, e6)], inc=cS)
                e8 = sch.emit("vector", lambda e: e.tensor_scalar(
                    out=rc[:, 4:5], in0=rc[:, 3:4], scalar1=1.0, scalar2=None, op0=ALU.add),
                    waits=[(cS, e7)], inc=vV)
                e9 = sch.emit("vector", lambda e: e.reciprocal(rc[:, 5:6], rc[:, 4:5]),
                              waits=[(vV, e8)], inc=vV)
                ea = sch.emit("vector", lambda e: e.tensor_scalar(
                    out=rc[:, 6:7], in0=rc[:, 5:6], scalar1=0.5, scalar2=None, op0=ALU.mult),
                    waits=[(vV, e9)], inc=vV)
                eb = sch.emit("vector", lambda e: e.tensor_scalar(
                    out=rc[:, 7:8], in0=rc[:, 6:7], scalar1=-1.0, scalar2=0.5,
                    op0=ALU.mult, op1=ALU.add), waits=[(vV, ea)], inc=vV)
                ec = sch.emit("vector", lambda e: e.tensor_scalar(
                    out=r8a[:, :], in0=r8a[:, :], scalar1=rc[:, 6:7], scalar2=None,
                    op0=ALU.mult), waits=[(vV, eb)], inc=vV)
                cvv = sch.emit("vector", lambda e, i=i: e.scalar_tensor_tensor(
                    out=comb[:, i * NE:(i + 1) * NE], in0=r8b[:, :], scalar=rc[:, 7:8],
                    in1=r8a[:, :], op0=ALU.mult, op1=ALU.add),
                    waits=[(vV, ec)], inc=vV)
                comb_vals[i] = (vV, cvv)
        return prev_mm

    def ffn_pre(xn_vals):
        pre = {}
        for w in (0, 1):
            cw0, wtlen = CHUNKS[w]
            lw = [xn_vals[(cw0 + wtlen - 1) // 128]]
            xv = None
            for j in range(ND):
                xv = sch.emit("scalar", lambda e_, j=j, c0=cw0, tlen=wtlen, b=xnTb[w % 2]: e_.dma_start_transpose(
                    out=b[:, j * TCH:j * TCH + tlen],
                    in_=XN[c0:c0 + tlen, j * 128:(j + 1) * 128]),
                    waits=(lw if j == 0 else []), inc=dC, by=16)
            pre[w] = xv
        return pre

    def ffn_call(xn_vals, comb_vals, w0_waits=(), pre=None):
        ys_tile_vals = [None] * NTT
        h_done = {}
        y_done = {}
        hevac = {}
        yevac_last = None
        store_last = None
        w1pv = {}
        w2pv = {}
        xv_of = dict(pre) if pre else {}
        units = [(e, ci) for e in range(NE) for ci in range(NCH)]

        def emit_xv(w):
            cw0, wtlen = CHUNKS[w % NCH]
            lw = [(tT, h_done[(w - 1) * NF - 1])] if w >= 2 else []
            lw.append(xn_vals[(cw0 + wtlen - 1) // 128])
            xv = None
            for j in range(ND):
                xv = sch.emit("scalar", lambda e_, j=j, c0=cw0, tlen=wtlen, b=xnTb[w % 2]: e_.dma_start_transpose(
                    out=b[:, j * TCH:j * TCH + tlen],
                    in_=XN[c0:c0 + tlen, j * 128:(j + 1) * 128]),
                    waits=(lw if j == 0 else []), inc=dC, by=16)
            xv_of[w] = xv

        if 0 not in xv_of:
            emit_xv(0)
        for u, (e, ci) in enumerate(units):
            c0, tlen = CHUNKS[ci]
            ntb = tlen // 128
            gfbase = u * NF
            if u + 1 < len(units) and (u + 1) not in xv_of:
                emit_xv(u + 1)
            xv = xv_of[u]
            if ci == 0:
                for q in range(8):
                    if e == 0:
                        ww1 = list(w0_waits)
                        ww2 = list(w0_waits)
                    else:
                        lastu = ((e - 1) * NCH + NCH - 1) * NF
                        ww1 = [(tT, h_done[lastu + q * 4 + 3])]
                        ww2 = [(tT, y_done[lastu + q * 4 + 3])]
                    w1pv[q] = sch.emit("gpsimd", lambda e_, e0=e, q=q: e_.dma_start(
                        out=w1sb[:, :].rearrange("p (j f) -> p j f", j=ND)[:, :, q * 512:(q + 1) * 512],
                        in_=w1_in[e0].rearrange("(j p) f -> p j f", p=128)[:, :, q * 512:(q + 1) * 512]),
                        waits=ww1, inc=dG, by=16)
                    w2pv[q] = sch.emit("gpsimd", lambda e_, e0=e, q=q: e_.dma_start(
                        out=w2sb[:, :].rearrange("p (k n) -> p k n", k=NF)[:, 4 * q:4 * q + 4, :],
                        in_=w2_in[e0].rearrange("(k p) n -> p k n", p=128)[:, 4 * q:4 * q + 4, :]),
                        waits=ww2, inc=dG, by=16)
            uw = [(dC, xv), (dG, w1pv[0])]
            if yevac_last is not None:
                uw.append((vV, yevac_last))
            for bk in range(6):
                uw.append(bank_wait(bk))
            for f in range(NF):
                gf = gfbase + f
                bk = 6 + (f % 2)
                hw = list(uw) if f == 0 else []
                if f > 0 and f % 4 == 0:
                    hw.append((dG, w1pv[f // 4]))
                if gf >= 2:
                    hw.append((cS, hevac[gf - 2]))
                hw.append(bank_wait(bk))

                def hmm(e_, f=f, tlen=tlen, xb=xnTb[u % 2], bk=bk):
                    ins = None
                    for j in range(ND):
                        ins = e_.matmul(
                            PS[bk][:, 0:tlen],
                            w1sb[:, (j * NF + f) * 128:(j * NF + f) * 128 + 128],
                            xb[:, j * TCH:j * TCH + tlen],
                            start=(j == 0), stop=(j == ND - 1))
                    return ins
                h_done[gf] = sch.emit("tensor", hmm, waits=hw, inc=tT)
                if f >= 1:
                    fy = f - 1
                    gfy = gfbase + fy
                    yw = [(cS, hevac[gfy])]
                    if fy % 4 == 0:
                        yw.append((dG, w2pv[fy // 4]))

                    def ymm(e_, fy=fy, ntb=ntb):
                        ins = None
                        for tb in range(ntb):
                            for dh in range(2):
                                ins = e_.matmul(
                                    PS[tb * 2 + dh][:, 0:512],
                                    hTr[:, (fy % 8) * TCH + tb * 128:(fy % 8) * TCH + tb * 128 + 128],
                                    w2sb[:, fy * D + dh * 512:fy * D + dh * 512 + 512],
                                    start=(fy == 0), stop=(fy == NF - 1))
                        return ins
                    y_done[gfy] = sch.emit("tensor", ymm, waits=yw, inc=tT)
                sw = [(tT, h_done[gf])]
                if gf >= 8:
                    sw.append((tT, y_done[gf - 8]))
                hevac[gf] = sch.emit("scalar", lambda e_, f=f, tlen=tlen, bk=bk: e_.activation(
                    out=hTr[:, (f % 8) * TCH:(f % 8) * TCH + tlen],
                    in_=PS[bk][:, 0:tlen], func=AF.Relu),
                    waits=sw, inc=cS)
                bank_set(bk, cS, hevac[gf])
            fy = NF - 1
            gfy = gfbase + fy

            def ymmt(e_, fy=fy, ntb=ntb):
                ins = None
                for tb in range(ntb):
                    for dh in range(2):
                        ins = e_.matmul(
                            PS[tb * 2 + dh][:, 0:512],
                            hTr[:, (fy % 8) * TCH + tb * 128:(fy % 8) * TCH + tb * 128 + 128],
                            w2sb[:, fy * D + dh * 512:fy * D + dh * 512 + 512],
                            start=(fy == 0), stop=(fy == NF - 1))
                return ins
            y_done[gfy] = sch.emit("tensor", ymmt, waits=[(cS, hevac[gfy])], inc=tT)
            evw = [(tT, y_done[gfy])]
            if store_last is not None:
                evw.append((dG, store_last))
            ev = None
            for tb in range(ntb):
                gi = (c0 + tb * 128) // 128
                cbw = [comb_vals[gi]] if comb_vals[gi] else []
                ev = sch.emit("vector", lambda e_, tb=tb, gi=gi, e0=e: e_.tensor_scalar(
                    out=ysb[:, tb * D:tb * D + 512], in0=PS[tb * 2][:, 0:512],
                    scalar1=comb[:, gi * NE + e0:gi * NE + e0 + 1], scalar2=None, op0=ALU.mult),
                    waits=(evw + cbw if tb == 0 else cbw), inc=vV)
                ev = sch.emit("vector", lambda e_, tb=tb, gi=gi, e0=e: e_.tensor_scalar(
                    out=ysb[:, tb * D + 512:tb * D + 1024], in0=PS[tb * 2 + 1][:, 0:512],
                    scalar1=comb[:, gi * NE + e0:gi * NE + e0 + 1], scalar2=None, op0=ALU.mult),
                    inc=vV)
            yevac_last = ev
            for bk in range(2 * ntb):
                bank_set(bk, vV, ev)
            acc = {} if e == 0 else dict(accum_op=ALU.add)
            store_last = sch.emit("gpsimd", lambda e_, c0=c0, tlen=tlen, ntb=ntb, acc=acc: e_.dma_start(
                out=YS[c0:c0 + tlen, :].rearrange("(tb p) n -> p tb n", p=128),
                in_=ysb[:, 0:ntb * D].rearrange("p (tb n) -> p tb n", tb=ntb), **acc),
                waits=[(vV, ev)], inc=dG, by=16)
            if e == NE - 1:
                for tb in range(ntb):
                    ys_tile_vals[(c0 + tb * 128) // 128] = (dG, store_last)
        return ys_tile_vals

    def attention_prologue(sv):
        v_msk = sch.emit("sync", lambda e: e.dma_start(
            out=msk_sb[:, :].rearrange("p (k s) -> p k s", k=4),
            in_=msk_in.rearrange("k p s -> p k s")), inc=dS, by=16)
        v_lnw = sch.emit("sync", lambda e: e.dma_start(out=lnw_sb[0:1, :], in_=lnw_in), inc=dS, by=16)
        prev_g2 = None
        for h_ in range(NH):
            gw_ = [(dS, v_lnw)]
            if prev_g2 is not None:
                gw_.append((tT, prev_g2))
            g1 = sch.emit("scalar", lambda e, h=h_: e.add(
                grow[0:1, :], lnw_sb[0:1, h * DH:(h + 1) * DH], 1.0),
                waits=gw_, inc=cS)
            g2 = sch.emit("tensor", lambda e: e.matmul(
                PS[1][:, 0:DH], ones1[0:1, :], grow[0:1, :], start=True, stop=True),
                waits=[(cS, g1), bank_wait(1)], inc=tT)
            g3 = sch.emit("vector", lambda e, h=h_: e.tensor_copy(
                gamma[:, h * DH:(h + 1) * DH], PS[1][:, 0:DH]),
                waits=[(tT, g2)], inc=vV)
            bank_set(1, vV, g3)
            prev_g2 = g2
        return dict(sv, v_msk=v_msk)

    def attention_batch(b, sv, xn_vals, st):
        att_vals = st["att_vals"]
        prev_head = st["prev_head"]
        prev_batch_om = st["prev_batch_om"]
        bS = b * S
        zw = [(tT, prev_batch_om)] if prev_batch_om else []
        zv = sch.emit("vector", lambda e: e.memset(zer4[0:NH, :], 0.0), waits=zw, inc=vV)
        prev_gmm = None
        igev = fgev = None
        for g in range(8):
            lw = [(tT, prev_gmm)] if prev_gmm else ([(tT, sch.tT.v)] if b > 0 else [])
            lw.append(xn_vals[b * 32 + g * 4 + 3])
            xv = None
            for j in range(ND):
                xv = sch.emit("sync", lambda e, g=g, j=j, bS=bS: e.dma_start_transpose(
                    out=xg_sb[:, j * 512:(j + 1) * 512],
                    in_=XN[bS + g * 512:bS + (g + 1) * 512, j * 128:(j + 1) * 128]),
                    waits=(lw if j == 0 else []), inc=dS, by=16)
            mi = None
            for j in range(ND):
                mi = sch.emit("tensor", lambda e, j=j: e.matmul(
                    PS[0][0:NH, :], s3aT[:, j * NH:(j + 1) * NH],
                    xg_sb[:, j * 512:(j + 1) * 512],
                    start=(j == 0), stop=(j == ND - 1)),
                    waits=([(dS, xv), bank_wait(0)] if j == 0 else []),
                    inc=(tT if j == ND - 1 else None))
            mf = None
            for j in range(ND):
                mf = sch.emit("tensor", lambda e, j=j: e.matmul(
                    PS[1][0:NH, :], s3fT[:, j * NH:(j + 1) * NH],
                    xg_sb[:, j * 512:(j + 1) * 512],
                    start=(j == 0), stop=(j == ND - 1)),
                    waits=([bank_wait(1)] if j == 0 else []),
                    inc=(tT if j == ND - 1 else None))
            prev_gmm = mf
            igev = sch.emit("vector", lambda e, g=g: e.tensor_scalar(
                out=ig4[0:NH, g * 512:(g + 1) * 512], in0=PS[0][0:NH, :],
                scalar1=bi_sb[0:NH, 0:1], scalar2=None, op0=ALU.add),
                waits=[(tT, mi), (dS, sv["v_bi"])], inc=vV)
            bank_set(0, vV, igev)
            fgev = sch.emit("vector", lambda e, g=g: e.tensor_scalar(
                out=fg4[0:NH, g * 512:(g + 1) * 512], in0=PS[1][0:NH, :],
                scalar1=bf_sb[0:NH, 0:1], scalar2=None, op0=ALU.add),
                waits=[(tT, mf), (dS, sv["v_bf"])], inc=vV)
            bank_set(1, vV, fgev)
        c1 = sch.emit("scalar", lambda e: e.activation(
            out=t4a[0:NH, :], in_=fg4[0:NH, :], func=AF.Abs),
            waits=[(vV, fgev)], inc=cS)
        c2 = sch.emit("scalar", lambda e: e.activation(
            out=t4a[0:NH, :], in_=t4a[0:NH, :], func=AF.Exp, scale=-1.0),
            waits=[(cS, c1)], inc=cS)
        c3 = sch.emit("scalar", lambda e: e.activation(
            out=t4a[0:NH, :], in_=t4a[0:NH, :], func=AF.Ln, bias=1.0),
            waits=[(cS, c2)], inc=cS)
        c4 = sch.emit("scalar", lambda e: e.activation(
            out=t4b[0:NH, :], in_=fg4[0:NH, :], func=AF.Relu, scale=-1.0),
            waits=[(cS, c3)], inc=cS)
        c5 = sch.emit("vector", lambda e: e.tensor_add(t4a[0:NH, :], t4a[0:NH, :], t4b[0:NH, :]),
                      waits=[(cS, c4)], inc=vV)
        c6 = sch.emit("vector", lambda e: e.tensor_scalar(
            out=t4a[0:NH, :], in0=t4a[0:NH, :], scalar1=-1.0, scalar2=None, op0=ALU.mult),
            waits=[(vV, c5)], inc=vV)
        c7 = sch.emit("vector", lambda e: e.tensor_tensor_scan(
            out=t4b[0:NH, :], data0=t4a[0:NH, :], data1=zer4[0:NH, :],
            initial=0.0, op0=ALU.add, op1=ALU.add),
            waits=[(vV, c6), (vV, zv)], inc=vV)
        c8 = sch.emit("vector", lambda e: e.tensor_sub(ig4[0:NH, :], ig4[0:NH, :], t4b[0:NH, :]),
                      waits=[(vV, c7), (vV, igev)], inc=vV)
        sa = sch.emit("sync", lambda e, b=b: e.dma_start(out=GS[b, 0], in_=ig4[0:NH, :]),
                      waits=[(vV, c8)], inc=dS, by=16)
        c9 = sch.emit("vector", lambda e: e.tensor_tensor_scan(
            out=t4a[0:NH, :], data0=ig4[0:NH, :], data1=ig4[0:NH, :],
            initial=-3.0e38, op0=ALU.max, op1=ALU.max),
            waits=[(vV, c8)], inc=vV)
        ca = sch.emit("vector", lambda e: e.tensor_scalar(
            out=fg4[0:NH, :], in0=t4a[0:NH, :], scalar1=-1.0, scalar2=None, op0=ALU.mult),
            waits=[(vV, c9)], inc=vV)
        sm = sch.emit("sync", lambda e, b=b: e.dma_start(out=GS[b, 1], in_=fg4[0:NH, :]),
                      waits=[(vV, ca)], inc=dS, by=16)
        cb = sch.emit("vector", lambda e: e.tensor_add(t4b[0:NH, :], t4b[0:NH, :], t4a[0:NH, :]),
                      waits=[(vV, c9)], inc=vV)
        cc = sch.emit("scalar", lambda e: e.activation(
            out=t4b[0:NH, :], in_=t4b[0:NH, :], func=AF.Exp, scale=-1.0),
            waits=[(vV, cb)], inc=cS)
        se = sch.emit("sync", lambda e, b=b: e.dma_start(out=GS[b, 2], in_=t4b[0:NH, :]),
                      waits=[(cS, cc)], inc=dS, by=16)

        for h in range(NH):
            raw = [(dS, sa), (vV, c9)]
            rmw = [(dS, sm)]
            rew = [(dS, se), (vV, cb)]
            xhw = []
            if prev_head is not None:
                raw.append((tT, prev_head["cm"]))
                rmw.append((tT, prev_head["mbm"]))
                rew.append((tT, prev_head["cm"]))
                xhw.append((tT, prev_head["last_mm"]))
            ra = sch.emit("sync", lambda e, b=b, h=h: e.dma_start(
                out=arow[0:1, :], in_=GS[b, 0, h].rearrange("(a s) -> a s", a=1)),
                waits=raw, inc=dS, by=16)
            rm = sch.emit("sync", lambda e, b=b, h=h: e.dma_start(
                out=mrow[0:1, :], in_=GS[b, 1, h].rearrange("(a s) -> a s", a=1)),
                waits=rmw, inc=dS, by=16)
            re_ = sch.emit("sync", lambda e, b=b, h=h: e.dma_start(
                out=erow[0:1, :], in_=GS[b, 2, h].rearrange("(a s) -> a s", a=1)),
                waits=rew, inc=dS, by=16)
            xh = None
            xhw.append(xn_vals[b * 32 + 31])
            for lc in range(2):
                xh = sch.emit("sync", lambda e, lc=lc, h=h, bS=bS: e.dma_start_transpose(
                    out=xhT[:, lc * S:(lc + 1) * S],
                    in_=XN[bS:bS + S, (2 * h + lc) * 128:(2 * h + lc) * 128 + 128]),
                    waits=(xhw if lc == 0 else []), inc=dS, by=16)

            def colmm(e_):
                ins = None
                for i in range(32):
                    ins = e_.matmul(PS[6][:, i:i + 1], arow[0:1, i * 128:(i + 1) * 128],
                                    ones1[0:1, 0:1], start=True, stop=True)
                for i in range(32):
                    ins = e_.matmul(PS[6][:, 32 + i:33 + i], erow[0:1, i * 128:(i + 1) * 128],
                                    ones1[0:1, 0:1], start=True, stop=True)
                return ins
            cm = sch.emit("tensor", colmm,
                          waits=[(dS, ra), (dS, re_), bank_wait(6)], inc=tT)
            cev = sch.emit("vector", lambda e: e.tensor_copy(cols[:, :], PS[6][:, 0:64]),
                           waits=[(tT, cm)], inc=vV)
            bank_set(6, vV, cev)
            qk_rot = [2, 3, 6, 7]
            qe = ke = ve = None
            last_mm = None
            rot = 0
            for lc in range(2):
                j2 = 2 * h + lc
                for grp in range(8):
                    bk = qk_rot[rot % 4]; rot += 1
                    qv = sch.emit("tensor", lambda e, j2=j2, lc=lc, grp=grp, bk=bk: e.matmul(
                        PS[bk][:, :], wqkvb[:, (0 * 8 + j2) * 128:(0 * 8 + j2) * 128 + 128],
                        xhT[:, lc * S + grp * 512:lc * S + (grp + 1) * 512],
                        start=True, stop=True),
                        waits=[(dS, xh), (dG, sv["v_wqkv"]), bank_wait(bk)], inc=tT)
                    qe = sch.emit("scalar", lambda e, lc=lc, grp=grp, bk=bk: e.copy(
                        qT[:, lc * S + grp * 512:lc * S + (grp + 1) * 512], PS[bk][:, :]),
                        waits=[(tT, qv)], inc=cS)
                    bank_set(bk, cS, qe)
                    bk = qk_rot[rot % 4]; rot += 1
                    kv = sch.emit("tensor", lambda e, j2=j2, lc=lc, grp=grp, bk=bk: e.matmul(
                        PS[bk][:, :], wqkvb[:, (1 * 8 + j2) * 128:(1 * 8 + j2) * 128 + 128],
                        xhT[:, lc * S + grp * 512:lc * S + (grp + 1) * 512],
                        start=True, stop=True),
                        waits=[bank_wait(bk)], inc=tT)
                    ke = sch.emit("vector", lambda e, lc=lc, grp=grp, bk=bk: e.tensor_scalar(
                        out=kT[:, lc * S + grp * 512:lc * S + (grp + 1) * 512],
                        in0=PS[bk][:, :], scalar1=1.0 / 16.0, scalar2=None, op0=ALU.mult),
                        waits=[(tT, kv)], inc=vV)
                    bank_set(bk, vV, ke)
                for i0 in range(0, 32, 4):
                    bk = 4 + (i0 // 4) % 2
                    def vmm(e_, j2=j2, lc=lc, i0=i0, bk=bk):
                        ins = None
                        for q in range(4):
                            ins = e_.matmul(
                                PS[bk][:, q * 128:(q + 1) * 128],
                                xhT[:, lc * S + (i0 + q) * 128:lc * S + (i0 + q + 1) * 128],
                                wqkvb[:, (2 * 8 + j2) * 128:(2 * 8 + j2) * 128 + 128],
                                start=True, stop=True)
                        return ins
                    vv_ = sch.emit("tensor", vmm, waits=[bank_wait(bk)], inc=tT)
                    last_mm = vv_
                    ve = sch.emit("scalar", lambda e, lc=lc, i0=i0, bk=bk,
                                  _w=None: e.copy(
                        v_sb[:, :].rearrange("p (i c) -> p i c", c=264)[:, i0:i0 + 4, lc * 128:lc * 128 + 128],
                        PS[bk][:, :].rearrange("p (i c) -> p i c", c=128)),
                        waits=[(tT, vv_), (vV, c7)], inc=cS)
                    bank_set(bk, cS, ve)
            von = sch.emit("vector", lambda e: e.memset(
                v_sb[:, :].rearrange("p (i c) -> p i c", c=264)[:, :, 256:257], 1.0),
                waits=[(cS, ve)], inc=vV)

            mbm_last = None
            hstore = [None, None]
            last_dv = None
            last_gadd = None
            for sS_ in range(8):
                mbm = sch.emit("tensor", lambda e, sS_=sS_: e.matmul(
                    PS[7][:, :], ones1[0:1, :], mrow[0:1, sS_ * 512:(sS_ + 1) * 512],
                    start=True, stop=True),
                    waits=[(dS, rm), bank_wait(7)], inc=tT)
                mbm_last = mbm
                mbw = [(tT, mbm)]
                if last_dv is not None:
                    mbw.append((cS, last_dv))
                if last_gadd is not None:
                    mbw.append((gC, last_gadd))
                mb0 = sch.emit("vector", lambda e: e.tensor_copy(mb_sb[:, 0:512], PS[7][:, :]),
                               waits=mbw, inc=vV)
                mbv = mb0
                gadd = None
                for k in range(4):
                    gadd = sch.emit("gpsimd", lambda e, k=k: e.tensor_add(
                        mb_sb[:, (k + 1) * 512:(k + 2) * 512], mb_sb[:, 0:512],
                        msk_sb[:, k * 512:(k + 1) * 512]),
                        waits=[(dS, sv["v_msk"]), (vV, mb0)], inc=gC)
                last_gadd = gadd
                bank_set(7, vV, mb0)
                ntile = 4 * (sS_ + 1)
                prevC = {0: None, 1: None}
                prev_out = {0: None, 1: None}
                pend_out = None

                def emit_out(ti, par, cv):
                    ow = [(vV, cv), (vV, von)]
                    if ti == 0:
                        for bk in range(4):
                            ow.append(bank_wait(bk))
                        ow.append((cS, ve))

                    def outmm(e_, ti=ti, par=par, last=(ti == ntile - 1)):
                        ins = None
                        for sb in range(4):
                            ins = e_.matmul(
                                PS[sb][:, 0:257], cT_sb[par][:, sb * 128:(sb + 1) * 128],
                                v_sb[:, ti * 264:ti * 264 + 257],
                                start=(ti == 0), stop=last)
                        return ins
                    return sch.emit("tensor", outmm, waits=ow, inc=tT)

                cv_of = {}
                om = None
                for ti in range(ntile):
                    par = ti % 2
                    qbk = 4 + par

                    def qkmm(e_, ti=ti, sS_=sS_, qbk=qbk):
                        ins = None
                        for lc in range(2):
                            ins = e_.matmul(
                                PS[qbk][:, :],
                                kT[:, lc * S + ti * 128:lc * S + (ti + 1) * 128],
                                qT[:, lc * S + sS_ * 512:lc * S + (sS_ + 1) * 512],
                                start=(lc == 0), stop=(lc == 1))
                        return ins
                    qw = [bank_wait(qbk)]
                    if sS_ == 0 and ti < 2:
                        qw += [(vV, ke), (cS, qe)]
                    qkv_ = sch.emit("tensor", qkmm, waits=qw, inc=tT)
                    if pend_out is not None:
                        pti, ppar = pend_out
                        om = emit_out(pti, ppar, cv_of[pti])
                        prev_out[ppar] = om
                    variant = 0 if ti < 4 * sS_ else 1 + (ti - 4 * sS_)
                    dwaits = [(vV, mb0), (vV, cev)]
                    if variant > 0:
                        dwaits.append((gC, gadd))
                    if prevC[par] is not None:
                        dwaits.append((vV, prevC[par]))
                    dv = sch.emit("scalar", lambda e, par=par, ti=ti, variant=variant: e.activation(
                        out=D_sb[par][:, :], in_=mb_sb[:, variant * 512:(variant + 1) * 512],
                        func=AF.Exp, bias=cols[:, ti:ti + 1]),
                        waits=dwaits, inc=cS)
                    last_dv = dv
                    cwaits = [(tT, qkv_), (cS, dv)]
                    if prev_out[par] is not None:
                        cwaits.append((tT, prev_out[par]))
                    cv = sch.emit("vector", lambda e, par=par, qbk=qbk: e.tensor_tensor(
                        cT_sb[par][:, :], PS[qbk][:, :], D_sb[par][:, :], ALU.mult),
                        waits=cwaits, inc=vV)
                    prevC[par] = cv
                    cv_of[ti] = cv
                    bank_set(qbk, vV, cv)
                    pend_out = (ti, par)
                pti, ppar = pend_out
                om = emit_out(pti, ppar, cv_of[pti])
                prev_out[ppar] = om
                for sb in range(4):
                    srow = bS + sS_ * 512 + sb * 128
                    gi = sS_ * 4 + sb
                    a0 = sb * 12
                    nb_ = no_sb[sb]
                    p1 = sch.emit("scalar", lambda e, sb=sb, a0=a0: e.activation(
                        out=ast[:, a0:a0 + 1], in_=PS[sb][:, 256:257], func=AF.Abs),
                        waits=[(tT, om)], inc=cS)
                    p2 = sch.emit("vector", lambda e, gi=gi, a0=a0: e.tensor_tensor(
                        ast[:, a0 + 1:a0 + 2], ast[:, a0:a0 + 1], cols[:, 32 + gi:33 + gi],
                        ALU.max), waits=[(cS, p1), (tT, om)], inc=vV)
                    p3 = sch.emit("vector", lambda e, a0=a0: e.tensor_scalar(
                        out=ast[:, a0 + 1:a0 + 2], in0=ast[:, a0 + 1:a0 + 2], scalar1=1e-8,
                        scalar2=None, op0=ALU.add), waits=[(vV, p2)], inc=vV)
                    p4 = sch.emit("vector", lambda e, a0=a0: e.reciprocal(
                        ast[:, a0 + 2:a0 + 3], ast[:, a0 + 1:a0 + 2]),
                        waits=[(vV, p3)], inc=vV)
                    p5 = sch.emit("vector", lambda e, sb=sb, a0=a0, nb_=nb_: e.tensor_scalar(
                        out=nb_[:, :], in0=PS[sb][:, 0:256], scalar1=ast[:, a0 + 2:a0 + 3],
                        scalar2=0.0, op0=ALU.mult, op1=ALU.add, accum_out=ast[:, a0 + 3:a0 + 4]),
                        waits=[(vV, p4)], inc=vV)
                    bank_set(sb, vV, p5)
                    p6 = sch.emit("scalar", lambda e, nb_=nb_, a0=a0: e.activation(
                        out=sq2[:, :], in_=nb_[:, :], func=AF.Square, accum_out=ast[:, a0 + 4:a0 + 5]),
                        waits=[(vV, p5)], inc=cS)
                    p7 = sch.emit("vector", lambda e, a0=a0: e.tensor_scalar(
                        out=ast[:, a0 + 5:a0 + 6], in0=ast[:, a0 + 3:a0 + 4], scalar1=1.0 / 256,
                        scalar2=None, op0=ALU.mult), waits=[(vV, p5)], inc=vV)
                    p8 = sch.emit("vector", lambda e, a0=a0: e.tensor_scalar(
                        out=ast[:, a0 + 6:a0 + 7], in0=ast[:, a0 + 4:a0 + 5], scalar1=1.0 / 256,
                        scalar2=None, op0=ALU.mult), waits=[(cS, p6)], inc=vV)
                    p9 = sch.emit("vector", lambda e, a0=a0: e.tensor_mul(
                        ast[:, a0 + 7:a0 + 8], ast[:, a0 + 5:a0 + 6], ast[:, a0 + 5:a0 + 6]),
                        waits=[(vV, p7)], inc=vV)
                    pa = sch.emit("vector", lambda e, a0=a0: e.tensor_sub(
                        ast[:, a0 + 8:a0 + 9], ast[:, a0 + 6:a0 + 7], ast[:, a0 + 7:a0 + 8]),
                        waits=[(vV, p8), (vV, p9)], inc=vV)
                    pb = sch.emit("scalar", lambda e, a0=a0: e.activation(
                        out=ast[:, a0 + 9:a0 + 10], in_=ast[:, a0 + 8:a0 + 9], func=AF.Ln,
                        bias=epsc[:, 0:1]), waits=[(vV, pa)], inc=cS)
                    pc = sch.emit("scalar", lambda e, a0=a0: e.activation(
                        out=ast[:, a0 + 10:a0 + 11], in_=ast[:, a0 + 9:a0 + 10], func=AF.Exp,
                        scale=-0.5), waits=[(cS, pb)], inc=cS)
                    pd = sch.emit("vector", lambda e, nb_=nb_, a0=a0: e.tensor_scalar(
                        out=nb_[:, :], in0=nb_[:, :], scalar1=ast[:, a0 + 5:a0 + 6],
                        scalar2=ast[:, a0 + 10:a0 + 11], op0=ALU.subtract, op1=ALU.mult),
                        waits=[(cS, pc)], inc=vV)
                    hpar = gi % 2
                    pew = [(vV, pd)]
                    if hstore[hpar] is not None:
                        pew.append((dS, hstore[hpar]))
                    pe_ = sch.emit("vector", lambda e, h=h, hpar=hpar, nb_=nb_: e.tensor_mul(
                        hout[hpar][:, :], nb_[:, :], gamma[:, h * DH:(h + 1) * DH]),
                        waits=pew, inc=vV)
                    hstore[hpar] = sch.emit("sync", lambda e, srow=srow, h=h, hpar=hpar: e.dma_start(
                        out=ATT[srow:srow + 128, h * DH:(h + 1) * DH], in_=hout[hpar][:, :]),
                        waits=[(vV, pe_)], inc=dS, by=16)
                    if h == NH - 1:
                        att_vals[srow // 128] = (dS, hstore[hpar])
            prev_head = dict(cm=cm, mbm=mbm_last, last_mm=last_mm)
            prev_batch_om = om
        st["prev_head"] = prev_head
        st["prev_batch_om"] = prev_batch_om

    def snap_all():
        return [(x, x.v) for x in sch.sems]

    def set_pend(snap):
        for e in ENGS:
            sch.pend[e] = sch.pend[e] + list(snap)

    sv = setup()
    xnv0 = [None] * NTT
    cmb0 = [None] * NTT
    sch.phase = "P0lnA"
    ln_pass(x_in, None, None, XN, xnv0, [None] * NTT, 0, 32)
    sch.phase = "P1pre"
    pre1 = ffn_pre(xnv0)
    sch.phase = "P0rA"
    rpm = router(sv, xnv0, cmb0, 0, 8, None, "sync", dS)
    sch.phase = "P0lnB"
    ln_pass(x_in, None, None, XN, xnv0, [None] * NTT, 32, 64)
    sch.phase = "P0rB"
    router(sv, xnv0, cmb0, 8, 16, rpm, "sync", dS)
    sch.phase = "P1ffn"
    ysv1 = ffn_call(xnv0, cmb0, w0_waits=[(tT, sv["t_setup"])], pre=pre1)
    snapP1 = snap_all()
    sch.phase = "P2ln"
    xnv2 = [None] * NTT
    x2v = [None] * NTT
    ln_pass(x_in, YS, X2, XN, xnv2, x2v, 0, 64, src1_vals=ysv1)
    set_pend(snapP1)
    sch.phase = "P3pro"
    sv = attention_prologue(sv)
    st = dict(att_vals=[None] * NTT, prev_head=None, prev_batch_om=None)
    sch.phase = "P3b0"
    attention_batch(0, sv, xnv2, st)
    sch.phase = "P3b1"
    attention_batch(1, sv, xnv2, st)
    attv = st["att_vals"]
    snapP3 = snap_all()
    set_pend(snapP3)
    sch.phase = "P4lnA"
    xnv4 = [None] * NTT
    x3v = [None] * NTT
    ln_pass(X2, ATT, X3, XN, xnv4, x3v, 0, 32, src0_vals=x2v, src1_vals=attv)
    sch.phase = "P4rA"
    rpm4 = router(sv, xnv4, cmb4 := [None] * NTT, 0, 8, None, "sync", dS)
    sch.phase = "P5pre"
    pre5 = ffn_pre(xnv4)
    sch.phase = "P4lnB"
    ln_pass(X2, ATT, X3, XN, xnv4, x3v, 32, 64, src0_vals=x2v, src1_vals=attv)
    sch.phase = "P4rB"
    router(sv, xnv4, cmb4, 8, 16, rpm4, "sync", dS)
    set_pend(snapP3)
    sch.phase = "P5ffn"
    ysv5 = ffn_call(xnv4, cmb4, pre=pre5)
    sch.phase = "P6ln"
    ln_pass(X3, YS, out_hb, None, [None] * NTT, [None] * NTT, 0, 64,
            src0_vals=x3v, src1_vals=ysv5)
    sch.emit("sync", lambda e: e.nop(),
             waits=[(sch.dS, sch.dS.v), (sch.dG, sch.dG.v), (sch.dV, sch.dV.v), (sch.dC, sch.dC.v)])

    with nc.Block() as block:
        @block.sync
        def _(e):
            sch.run("sync", e)

        @block.gpsimd
        def _(e):
            sch.run("gpsimd", e)

        @block.tensor
        def _(e):
            sch.run("tensor", e)

        @block.vector
        def _(e):
            sch.run("vector", e)

        @block.scalar
        def _(e):
            sch.run("scalar", e)

    for s in sch.sems:
        s.cm.__exit__(None, None, None)
    return nc


def _blockdiag_blocks(w):
    out = np.zeros((8, 128, 128), dtype=np.float32)
    for j in range(8):
        for g in range(32):
            out[j, 4 * g:4 * g + 4, 4 * g:4 * g + 4] = w[32 * j + g].T
    return out


def kernel(L0_attn_x, L0_attn_wq, L0_attn_wk, L0_attn_wv, L0_attn_wi, L0_attn_bi,
           L0_attn_wf, L0_attn_bf, L0_attn_ln_w, L0_ffn_gate_w, L0_ffn_w1, L0_ffn_w2):
    x = np.ascontiguousarray(np.asarray(L0_attn_x, dtype=np.float32))
    wqkv = np.concatenate([_blockdiag_blocks(np.asarray(w, dtype=np.float32))
                           for w in (L0_attn_wq, L0_attn_wk, L0_attn_wv)], axis=0)
    msk = np.zeros((4, 128, 512), dtype=np.float32)
    for k in range(4):
        ti = np.arange(128)[:, None] + k * 128
        sj = np.arange(512)[None, :]
        msk[k] = np.where(ti <= sj, 0.0, -1e30)
    shared = {
        "w1": np.ascontiguousarray(np.asarray(L0_ffn_w1, dtype=np.float32)),
        "w2": np.ascontiguousarray(np.asarray(L0_ffn_w2, dtype=np.float32)),
        "gw": np.ascontiguousarray(np.asarray(L0_ffn_gate_w, dtype=np.float32)),
        "wqkv": wqkv,
        "wi": np.ascontiguousarray(np.asarray(L0_attn_wi, dtype=np.float32)),
        "wf": np.ascontiguousarray(np.asarray(L0_attn_wf, dtype=np.float32)),
        "bi": np.asarray(L0_attn_bi, dtype=np.float32).reshape(NH, 1),
        "bfg": np.asarray(L0_attn_bf, dtype=np.float32).reshape(NH, 1),
        "lnw": np.asarray(L0_attn_ln_w, dtype=np.float32).reshape(1, D),
        "msk": msk,
        "idn": np.eye(128, dtype=np.float32),
    }
    nc = build_core_program()
    in_maps = []
    for c in range(8):
        m = dict(shared)
        m["x"] = np.ascontiguousarray(x[2 * c:2 * c + 2].reshape(T, D))
        in_maps.append(m)
    res = run_bass_kernel_spmd(nc, in_maps, core_ids=list(range(8)))
    out = np.concatenate([res.results[c]["out"].reshape(2, S, D) for c in range(8)], axis=0)
    return out


# version 10
# speedup vs baseline: 1.0182x; 1.0182x over previous
import numpy as np
import ml_dtypes
import concourse.bass as bass
import concourse.mybir as mybir
from concourse.bass_utils import run_bass_kernel_spmd

F32 = mybir.dt.float32
BF16 = mybir.dt.bfloat16
AX = mybir.AxisListType.X
ALU = mybir.AluOpType
AF = mybir.ActivationFunctionType

NB = 2
S = 4096
T = NB * S
D = 1024
FF = 4096
NE = 8
NH = 4
DH = 256
ND = D // 128
NF = FF // 128
TCH = 384
CHUNKS = [(s0, min(TCH, T - s0)) for s0 in range(0, T, TCH)]
NCH = len(CHUNKS)
NTT = T // 128
ENGS = ("sync", "gpsimd", "tensor", "vector", "scalar")
DEBUG_OUTPUTS = False


class Sem:
    def __init__(self, nc, name):
        self.cm = nc.semaphore(name)
        self.h = self.cm.__enter__()
        self.v = 0


class Sch:

    def __init__(self, nc):
        self.nc = nc
        self.phase = "setup"
        self.q = {e: [] for e in ENGS}
        self.wm = {e: {} for e in ENGS}
        self.pend = {e: [] for e in ENGS}
        self.dS = Sem(nc, "dS")
        self.dG = Sem(nc, "dG")
        self.dV = Sem(nc, "dV")
        self.dC = Sem(nc, "dC")
        self.tT = Sem(nc, "tT")
        self.vV = Sem(nc, "vV")
        self.cS = Sem(nc, "cS")
        self.gC = Sem(nc, "gC")
        self.sems = [self.dS, self.dG, self.dV, self.dC, self.tT, self.vV, self.cS, self.gC]
        self.bank_free = {i: None for i in range(8)}
        self.ln_last = [None, None]
        self.ln_stat = [None, None]

    def emit(self, eng, fn, waits=(), inc=None, by=1):
        ws = []
        for w in list(self.pend[eng]) + list(waits):
            if w is None:
                continue
            s, v = w
            if v > self.wm[eng].get(id(s), 0):
                ws.append((s, v))
                self.wm[eng][id(s)] = v
        self.pend[eng] = []
        val = None
        if inc is not None:
            inc.v += by
            val = inc.v
        self.q[eng].append((ws, fn, inc, by, self.phase))
        return val

    def barrier(self):
        snap = [(s, s.v) for s in self.sems]
        for e in ENGS:
            self.pend[e] = list(snap)

    def run(self, eng, eobj):
        cur = None
        sid = None
        for ws, fn, inc, by, ph in self.q[eng]:
            if ph != cur:
                if cur is not None:
                    self.nc.leave_named_scope(cur, sid, False)
                sid, _ = self.nc.enter_named_scope(ph, False)
                cur = ph
            for s, v in ws:
                eobj.wait_ge(s.h, v)
            ins = fn(eobj)
            if inc is not None:
                ins.then_inc(inc.h, by)
        if cur is not None:
            self.nc.leave_named_scope(cur, sid, False)


def build_core_program():
    nc = bass.Bass("TRN2", target_bir_lowering=False)
    x_in = nc.dram_tensor("x", [T, D], F32, kind="ExternalInput").ap()
    w1_in = nc.dram_tensor("w1", [NE, D, FF], F32, kind="ExternalInput").ap()
    w2_in = nc.dram_tensor("w2", [NE, FF, D], F32, kind="ExternalInput").ap()
    gw_in = nc.dram_tensor("gw", [D, NE], F32, kind="ExternalInput").ap()
    wqkv_in = nc.dram_tensor("wqkv", [24, 128, 128], F32, kind="ExternalInput").ap()
    wi_in = nc.dram_tensor("wi", [NH, 3 * D], F32, kind="ExternalInput").ap()
    wf_in = nc.dram_tensor("wf", [NH, 3 * D], F32, kind="ExternalInput").ap()
    bi_in = nc.dram_tensor("bi", [NH, 1], F32, kind="ExternalInput").ap()
    bf_in = nc.dram_tensor("bfg", [NH, 1], F32, kind="ExternalInput").ap()
    lnw_in = nc.dram_tensor("lnw", [1, D], F32, kind="ExternalInput").ap()
    msk_in = nc.dram_tensor("msk", [4, 128, 512], F32, kind="ExternalInput").ap()
    idn_in = nc.dram_tensor("idn", [128, 128], F32, kind="ExternalInput").ap()
    out_hb = nc.dram_tensor("out", [T, D], F32, kind="ExternalOutput").ap()
    K_ = "ExternalOutput" if DEBUG_OUTPUTS else "Internal"
    XN = nc.dram_tensor("XN", [T, D], BF16, kind=K_).ap()
    X2 = nc.dram_tensor("X2", [T, D], F32, kind=K_).ap()
    X3 = nc.dram_tensor("X3", [T, D], F32, kind=K_).ap()
    ATT = nc.dram_tensor("ATT", [T, D], F32, kind=K_).ap()
    YS = nc.dram_tensor("YS", [T, D], F32, kind=K_).ap()
    GS = nc.dram_tensor("GS", [NB, 3, NH, S], F32, kind=K_).ap()

    off = [17408]

    def P(name, shape, dt):
        h = nc.alloc_sbuf_tensor_at(name, shape, dt, offset=off[0])
        nbytes = int(np.prod(shape[1:])) * (2 if dt == BF16 else 4)
        off[0] += ((nbytes + 31) // 32) * 32
        return h

    gwb = P("gwb", [128, ND * NE], BF16)
    wqkvb = P("wqkvb", [128, 24 * 128], BF16)
    s3aT = P("s3aT", [128, ND * NH], BF16)
    s3fT = P("s3fT", [128, ND * NH], BF16)
    bi_sb = P("bi_sb", [NH, 1], F32)
    bf_sb = P("bf_sb", [NH, 1], F32)
    ones1 = P("ones1", [1, 128], F32)
    comb = P("comb", [128, NTT * NE], F32)
    grow = P("grow", [1, 256], F32)
    epsc = P("epsc", [128, 1], F32)
    idn_sb = P("idn_sb", [128, 128], F32)
    PERS = ((off[0] + 31) // 32) * 32

    F0 = PERS
    R0 = F0 + 161792
    L0 = R0 + 10368

    def A(name, shape, dt, at):
        return nc.alloc_sbuf_tensor_at(name, shape, dt, offset=F0 + at)

    w1sb = A("w1sb", [128, ND * FF], BF16, 0)
    w2sb = A("w2sb", [128, NF * D], BF16, 65536)
    xnTb = [A(f"xnT{i}", [128, ND * TCH], BF16, 131072 + i * 6144) for i in range(2)]
    hTr = A("hTr", [128, 8 * TCH], BF16, 143360)
    ysb = A("ysb", [128, 3 * D], F32, 149504)
    xg_sb = A("xg", [128, ND * 512], BF16, 161792)
    glT = A("glT", [NE, 512], F32, 169984)
    gl_t = A("gl_t", [128, NE], F32, 172032)
    r8a = A("r8a", [128, NE], F32, 172064)
    r8b = A("r8b", [128, NE], F32, 172096)
    rc = A("rc", [128, 8], F32, 172128)
    xt2 = [A(f"xt{i}", [128, D], F32, 172160 + i * 4096) for i in range(2)]
    yt2 = [A(f"yt{i}", [128, D], F32, 180352 + i * 4096) for i in range(2)]
    xnb2 = [A(f"xnb{i}", [128, D], BF16, 188544 + i * 2048) for i in range(2)]
    sqsc = A("sqsc", [128, D], F32, 192640)
    stat = A("stat", [128, 16], F32, 196736)
    xhT = A("xhT", [128, 2 * S], BF16, 0)
    qT = A("qT", [128, 2 * S], BF16, 16384)
    kT = A("kT", [128, 2 * S], BF16, 32768)
    zer4 = A("zer4", [NH, S], F32, 49152)
    v_sb = A("v_sb", [128, 32 * 264], BF16, 49152)
    mb_sb = A("mb_sb", [128, 5 * 512], F32, 66048)
    D_sb = [A(f"Dsb{i}", [128, 512], F32, 76288 + i * 2048) for i in range(2)]
    cT_sb = [A(f"cT{i}", [128, 512], BF16, 80384 + i * 1024) for i in range(2)]
    cols = A("cols", [128, 64], F32, 82432)
    hout = [A(f"hout{i}", [128, DH], F32, 82688 + i * 1024) for i in range(2)]
    no_sb = [A(f"no_sb{i}", [128, DH], F32, 84736 + i * 1024) for i in range(4)]
    sq2 = A("sq2", [128, DH], F32, 88832)
    ast = A("ast", [128, 64], F32, 89856)
    msk_sb = A("msk_sb", [128, 4 * 512], F32, 90112)
    lnw_sb = A("lnw_sb", [1, D], F32, 98304)
    gamma = A("gamma", [128, NH * DH], F32, 102400)
    ig4 = A("ig4", [NH, S], F32, 106496)
    fg4 = A("fg4", [NH, S], F32, 122880)
    t4a = A("t4a", [NH, S], F32, 139264)
    t4b = A("t4b", [NH, S], F32, 155648)
    arow = A("arow", [1, S], F32, 106496)
    mrow = A("mrow", [1, S], F32, 122880)
    erow = A("erow", [1, S], F32, 139264)
    s3w = A("s3w", [NH, 3 * D], F32, 0)
    s3s = A("s3s", [NH, D], F32, 12288)

    PS = [nc.alloc_psum_tensor(f"ps{i}", [128, 512], F32) for i in range(8)]

    sch = Sch(nc)
    dS, dG, dV, dC = sch.dS, sch.dG, sch.dV, sch.dC
    tT, vV, cS, gC = sch.tT, sch.vV, sch.cS, sch.gC
    bank_wait = lambda i: sch.bank_free[i]

    def bank_set(i, sem, val):
        sch.bank_free[i] = (sem, val)

    def setup():
        v_idn = sch.emit("sync", lambda e: e.dma_start(out=idn_sb[:, :], in_=idn_in), inc=dS, by=16)
        v_bi = sch.emit("sync", lambda e: e.dma_start(out=bi_sb[:, :], in_=bi_in), inc=dS, by=16)
        v_bf = sch.emit("sync", lambda e: e.dma_start(out=bf_sb[:, :], in_=bf_in), inc=dS, by=16)
        v_gw = sch.emit("gpsimd", lambda e: e.dma_start(
            out=gwb[:, :].rearrange("p (j e) -> p j e", j=ND),
            in_=gw_in.rearrange("(j p) e -> p j e", p=128)), inc=dG, by=16)
        v_wqkv = sch.emit("gpsimd", lambda e: e.dma_start(
            out=wqkvb[:, :].rearrange("p (m c) -> p m c", m=24),
            in_=wqkv_in.rearrange("m p c -> p m c")), inc=dG, by=16)
        v_o1 = sch.emit("vector", lambda e: e.memset(ones1[0:1, :], 1.0), inc=vV)
        v_eps = sch.emit("vector", lambda e: e.memset(epsc[:, :], 1e-5), inc=vV)

        prev_tv = None
        prev_a2 = None
        for which in range(2):
            src = wi_in if which == 0 else wf_in
            dstT = s3aT if which == 0 else s3fT
            lw = []
            if which == 1:
                lw = [(vV, prev_a2), (tT, prev_tv)]
            wv = sch.emit("sync", lambda e, src=src: e.dma_start(out=s3w[0:NH, :], in_=src),
                          waits=lw, inc=dS, by=16)
            a1 = sch.emit("vector", lambda e: e.tensor_add(s3s[0:NH, :], s3w[0:NH, 0:D], s3w[0:NH, D:2 * D]),
                          waits=[(dS, wv)], inc=vV)
            a2 = sch.emit("vector", lambda e: e.tensor_add(s3s[0:NH, :], s3s[0:NH, :], s3w[0:NH, 2 * D:3 * D]),
                          waits=[(vV, a1)], inc=vV)
            prev_a2 = a2
            for j in range(ND):
                tv = sch.emit("tensor", lambda e, j=j: e.transpose(
                    PS[0][0:128, 0:NH], s3s[0:NH, j * 128:(j + 1) * 128], idn_sb[0:NH, 0:NH]),
                    waits=[(vV, a2), (dS, v_idn), bank_wait(0)], inc=tT)
                prev_tv = tv
                ev = sch.emit("vector", lambda e, j=j, d=dstT: e.tensor_copy(
                    d[:, j * NH:(j + 1) * NH], PS[0][0:128, 0:NH]),
                    waits=[(tT, tv)], inc=vV)
                bank_set(0, vV, ev)
        return dict(v_gw=v_gw, v_wqkv=v_wqkv, v_bi=v_bi, v_bf=v_bf, v_eps=v_eps,
                    v_idn=v_idn, v_o1=v_o1, t_setup=sch.tT.v, v_setup=sch.vV.v)

    def ln_maker(src0, src1, out_x, out_xn, xn_vals, x_vals,
                 src0_vals=None, src1_vals=None):
        def do_tile(i):
            p = i % 2
            so8 = p * 8
            r0 = i * 128
            lw = list(sch.ln_last[p]) if sch.ln_last[p] else []
            if src0_vals is not None:
                lw.append(src0_vals[i])
            vx = sch.emit("sync", lambda e, o=xt2[p], s=src0, r=r0: e.dma_start(
                out=o[:, :], in_=s[r:r + 128, :]), waits=lw, inc=dS, by=16)
            statw = [(vV, sch.ln_stat[p])] if sch.ln_stat[p] else []
            if src1 is not None:
                lw2 = [src1_vals[i]] if src1_vals is not None else []
                vy = sch.emit("sync", lambda e, o=yt2[p], s=src1, r=r0: e.dma_start(
                    out=o[:, :], in_=s[r:r + 128, :]), waits=lw2, inc=dS, by=16)
                vadd = sch.emit("vector", lambda e, p=p, so8=so8: e.scalar_tensor_tensor(
                    out=xt2[p][:, :], in0=xt2[p][:, :], scalar=1.0, in1=yt2[p][:, :],
                    op0=ALU.mult, op1=ALU.add, accum_out=stat[:, so8:so8 + 1]),
                    waits=[(dS, vx), (dS, vy)] + statw, inc=vV)
                xready = (vV, vadd)
            else:
                vsum = sch.emit("vector", lambda e, p=p, so8=so8: e.tensor_scalar(
                    out=sqsc[:, :], in0=xt2[p][:, :], scalar1=1.0, scalar2=0.0,
                    op0=ALU.mult, op1=ALU.add, accum_out=stat[:, so8:so8 + 1]),
                    waits=[(dS, vx)] + statw, inc=vV)
                xready = (vV, vsum)
            consumers = []
            if out_x is not None:
                so = sch.emit("sync", lambda e, p=p, r=r0, o=out_x: e.dma_start(
                    out=o[r:r + 128, :], in_=xt2[p][:, :]), waits=[xready], inc=dS, by=16)
                consumers.append((dS, so))
                x_vals[i] = (dS, so)
            if out_xn is not None:
                vsq = sch.emit("scalar", lambda e, p=p, so8=so8: e.activation(
                    out=sqsc[:, :], in_=xt2[p][:, :], func=AF.Square,
                    accum_out=stat[:, so8 + 1:so8 + 2]),
                    waits=[xready], inc=cS)
                m1 = sch.emit("vector", lambda e, so8=so8: e.tensor_scalar(
                    out=stat[:, so8 + 2:so8 + 3], in0=stat[:, so8:so8 + 1],
                    scalar1=1.0 / D, scalar2=None, op0=ALU.mult),
                    waits=[xready], inc=vV)
                m2 = sch.emit("vector", lambda e, so8=so8: e.tensor_scalar(
                    out=stat[:, so8 + 3:so8 + 4], in0=stat[:, so8 + 1:so8 + 2],
                    scalar1=1.0 / D, scalar2=None, op0=ALU.mult),
                    waits=[(cS, vsq)], inc=vV)
                m3 = sch.emit("vector", lambda e, so8=so8: e.tensor_mul(
                    stat[:, so8 + 4:so8 + 5], stat[:, so8 + 2:so8 + 3], stat[:, so8 + 2:so8 + 3]),
                    waits=[(vV, m1)], inc=vV)
                m4 = sch.emit("vector", lambda e, so8=so8: e.tensor_sub(
                    stat[:, so8 + 5:so8 + 6], stat[:, so8 + 3:so8 + 4], stat[:, so8 + 4:so8 + 5]),
                    waits=[(vV, m2), (vV, m3)], inc=vV)
                l1 = sch.emit("scalar", lambda e, so8=so8: e.activation(
                    out=stat[:, so8 + 6:so8 + 7], in_=stat[:, so8 + 5:so8 + 6],
                    func=AF.Ln, bias=epsc[:, 0:1]),
                    waits=[(vV, m4)], inc=cS)
                l2 = sch.emit("scalar", lambda e, so8=so8: e.activation(
                    out=stat[:, so8 + 7:so8 + 8], in_=stat[:, so8 + 6:so8 + 7],
                    func=AF.Exp, scale=-0.5), waits=[(cS, l1)], inc=cS)
                vn = sch.emit("vector", lambda e, p=p, so8=so8: e.tensor_scalar(
                    out=xnb2[p][:, :], in0=xt2[p][:, :], scalar1=stat[:, so8 + 2:so8 + 3],
                    scalar2=stat[:, so8 + 7:so8 + 8], op0=ALU.subtract, op1=ALU.mult),
                    waits=[(cS, l2), (vV, m1)], inc=vV)
                sn = sch.emit("sync", lambda e, p=p, r=r0, o=out_xn: e.dma_start(
                    out=o[r:r + 128, :], in_=xnb2[p][:, :]), waits=[(vV, vn)], inc=dS, by=16)
                consumers.append((dS, sn))
                consumers.append((vV, vn))
                sch.ln_stat[p] = vn
                xn_vals[i] = (dS, sn)
            else:
                sch.ln_stat[p] = xready[1]
            if not consumers:
                consumers.append(xready)
            sch.ln_last[p] = consumers
        return do_tile

    def router_g(sv, g, xn_vals, comb_vals, rst):
        lw = [(tT, rst["prev_mm"])] if rst.get("prev_mm") else []
        lw.append(xn_vals[g * 4 + 3])
        xv = None
        for j in range(ND):
            xv = sch.emit("sync", lambda e, g=g, j=j: e.dma_start_transpose(
                out=xg_sb[:, j * 512:(j + 1) * 512],
                in_=XN[g * 512:(g + 1) * 512, j * 128:(j + 1) * 128]),
                waits=(lw if j == 0 else []), inc=dS, by=16)
        mmv = None
        for j in range(ND):
            mmv = sch.emit("tensor", lambda e, j=j: e.matmul(
                PS[0][0:NE, :], gwb[:, j * NE:(j + 1) * NE], xg_sb[:, j * 512:(j + 1) * 512],
                start=(j == 0), stop=(j == ND - 1)),
                waits=([(dS, xv), (dG, sv["v_gw"]), bank_wait(0)] if j == 0 else []),
                inc=(tT if j == ND - 1 else None))
        rst["prev_mm"] = mmv
        ev = sch.emit("vector", lambda e: e.tensor_copy(glT[0:NE, :], PS[0][0:NE, :]),
                      waits=[(tT, mmv)], inc=vV)
        bank_set(0, vV, ev)
        for q in range(4):
            i = g * 4 + q
            tv = sch.emit("tensor", lambda e, q=q: e.transpose(
                PS[1][0:128, 0:NE], glT[0:NE, q * 128:(q + 1) * 128], idn_sb[0:NE, 0:NE]),
                waits=[(vV, ev), bank_wait(1)], inc=tT)
            e0 = sch.emit("vector", lambda e: e.tensor_copy(gl_t[:, :], PS[1][0:128, 0:NE]),
                          waits=[(tT, tv)], inc=vV)
            bank_set(1, vV, e0)
            e1 = sch.emit("vector", lambda e: e.reduce_max(rc[:, 0:1], gl_t[:, :], axis=AX),
                          waits=[(vV, e0)], inc=vV)
            e2 = sch.emit("vector", lambda e: e.tensor_scalar(
                out=r8a[:, :], in0=gl_t[:, :], scalar1=rc[:, 0:1], scalar2=None,
                op0=ALU.is_equal), waits=[(vV, e1)], inc=vV)
            e3 = sch.emit("vector", lambda e: e.scalar_tensor_tensor(
                out=r8b[:, :], in0=r8a[:, :], scalar=-1e30, in1=gl_t[:, :],
                op0=ALU.mult, op1=ALU.add), waits=[(vV, e2)], inc=vV)
            e4 = sch.emit("vector", lambda e: e.reduce_max(rc[:, 1:2], r8b[:, :], axis=AX),
                          waits=[(vV, e3)], inc=vV)
            e5 = sch.emit("vector", lambda e: e.tensor_scalar(
                out=r8b[:, :], in0=r8b[:, :], scalar1=rc[:, 1:2], scalar2=None,
                op0=ALU.is_equal), waits=[(vV, e4)], inc=vV)
            e6 = sch.emit("vector", lambda e: e.tensor_sub(rc[:, 2:3], rc[:, 1:2], rc[:, 0:1]),
                          waits=[(vV, e5)], inc=vV)
            e7 = sch.emit("scalar", lambda e: e.activation(
                out=rc[:, 3:4], in_=rc[:, 2:3], func=AF.Exp), waits=[(vV, e6)], inc=cS)
            e8 = sch.emit("vector", lambda e: e.tensor_scalar(
                out=rc[:, 4:5], in0=rc[:, 3:4], scalar1=1.0, scalar2=None, op0=ALU.add),
                waits=[(cS, e7)], inc=vV)
            e9 = sch.emit("vector", lambda e: e.reciprocal(rc[:, 5:6], rc[:, 4:5]),
                          waits=[(vV, e8)], inc=vV)
            ea = sch.emit("vector", lambda e: e.tensor_scalar(
                out=rc[:, 6:7], in0=rc[:, 5:6], scalar1=0.5, scalar2=None, op0=ALU.mult),
                waits=[(vV, e9)], inc=vV)
            eb = sch.emit("vector", lambda e: e.tensor_scalar(
                out=rc[:, 7:8], in0=rc[:, 6:7], scalar1=-1.0, scalar2=0.5,
                op0=ALU.mult, op1=ALU.add), waits=[(vV, ea)], inc=vV)
            ec = sch.emit("vector", lambda e: e.tensor_scalar(
                out=r8a[:, :], in0=r8a[:, :], scalar1=rc[:, 6:7], scalar2=None,
                op0=ALU.mult), waits=[(vV, eb)], inc=vV)
            cvv = sch.emit("vector", lambda e, i=i: e.scalar_tensor_tensor(
                out=comb[:, i * NE:(i + 1) * NE], in0=r8b[:, :], scalar=rc[:, 7:8],
                in1=r8a[:, :], op0=ALU.mult, op1=ALU.add),
                waits=[(vV, ec)], inc=vV)
            comb_vals[i] = (vV, cvv)

    def ffn_pre(xn_vals):
        pre = {}
        for w in (0, 1):
            cw0, wtlen = CHUNKS[w]
            lw = [xn_vals[(cw0 + wtlen - 1) // 128]]
            xv = None
            for j in range(ND):
                xv = sch.emit("scalar", lambda e_, j=j, c0=cw0, tlen=wtlen, b=xnTb[w % 2]: e_.dma_start_transpose(
                    out=b[:, j * TCH:j * TCH + tlen],
                    in_=XN[c0:c0 + tlen, j * 128:(j + 1) * 128]),
                    waits=(lw if j == 0 else []), inc=dC, by=16)
            pre[w] = xv
        return pre

    def ffn_call(xn_vals, comb_vals, ys_tile_vals, w0_waits=(), pre=None,
                 in_cb=None, out_cb=None):
        h_done = {}
        y_done = {}
        hevac = {}
        yevac_last = None
        store_last = None
        w1pv = {}
        w2pv = {}
        xv_of = dict(pre) if pre else {}
        units = [(e, ci) for e in range(NE) for ci in range(NCH)]

        def emit_xv(w):
            cw0, wtlen = CHUNKS[w % NCH]
            lw = [(tT, h_done[(w - 1) * NF - 1])] if w >= 2 else []
            lw.append(xn_vals[(cw0 + wtlen - 1) // 128])
            xv = None
            for j in range(ND):
                xv = sch.emit("scalar", lambda e_, j=j, c0=cw0, tlen=wtlen, b=xnTb[w % 2]: e_.dma_start_transpose(
                    out=b[:, j * TCH:j * TCH + tlen],
                    in_=XN[c0:c0 + tlen, j * 128:(j + 1) * 128]),
                    waits=(lw if j == 0 else []), inc=dC, by=16)
            xv_of[w] = xv

        if in_cb:
            in_cb((CHUNKS[0][0] + CHUNKS[0][1] - 1) // 128)
        if 0 not in xv_of:
            emit_xv(0)
        for u, (e, ci) in enumerate(units):
            c0, tlen = CHUNKS[ci]
            ntb = tlen // 128
            gfbase = u * NF
            if u + 1 < len(units):
                if in_cb:
                    c1_, t1_ = CHUNKS[(u + 1) % NCH]
                    in_cb((c1_ + t1_ - 1) // 128)
                if (u + 1) not in xv_of:
                    emit_xv(u + 1)
            xv = xv_of[u]
            if ci == 0:
                for q in range(8):
                    if e == 0:
                        ww1 = list(w0_waits)
                        ww2 = list(w0_waits)
                    else:
                        lastu = ((e - 1) * NCH + NCH - 1) * NF
                        ww1 = [(tT, h_done[lastu + q * 4 + 3])]
                        ww2 = [(tT, y_done[lastu + q * 4 + 3])]
                    w1pv[q] = sch.emit("gpsimd", lambda e_, e0=e, q=q: e_.dma_start(
                        out=w1sb[:, :].rearrange("p (j f) -> p j f", j=ND)[:, :, q * 512:(q + 1) * 512],
                        in_=w1_in[e0].rearrange("(j p) f -> p j f", p=128)[:, :, q * 512:(q + 1) * 512]),
                        waits=ww1, inc=dG, by=16)
                    w2pv[q] = sch.emit("gpsimd", lambda e_, e0=e, q=q: e_.dma_start(
                        out=w2sb[:, :].rearrange("p (k n) -> p k n", k=NF)[:, 4 * q:4 * q + 4, :],
                        in_=w2_in[e0].rearrange("(k p) n -> p k n", p=128)[:, 4 * q:4 * q + 4, :]),
                        waits=ww2, inc=dG, by=16)
            uw = [(dC, xv), (dG, w1pv[0])]
            if yevac_last is not None:
                uw.append((vV, yevac_last))
            for bk in range(6):
                uw.append(bank_wait(bk))
            for f in range(NF):
                gf = gfbase + f
                bk = 6 + (f % 2)
                hw = list(uw) if f == 0 else []
                if f > 0 and f % 4 == 0:
                    hw.append((dG, w1pv[f // 4]))
                if gf >= 2:
                    hw.append((cS, hevac[gf - 2]))
                hw.append(bank_wait(bk))

                def hmm(e_, f=f, tlen=tlen, xb=xnTb[u % 2], bk=bk):
                    ins = None
                    for j in range(ND):
                        ins = e_.matmul(
                            PS[bk][:, 0:tlen],
                            w1sb[:, (j * NF + f) * 128:(j * NF + f) * 128 + 128],
                            xb[:, j * TCH:j * TCH + tlen],
                            start=(j == 0), stop=(j == ND - 1))
                    return ins
                h_done[gf] = sch.emit("tensor", hmm, waits=hw, inc=tT)
                if f >= 1:
                    fy = f - 1
                    gfy = gfbase + fy
                    yw = [(cS, hevac[gfy])]
                    if fy % 4 == 0:
                        yw.append((dG, w2pv[fy // 4]))

                    def ymm(e_, fy=fy, ntb=ntb):
                        ins = None
                        for tb in range(ntb):
                            for dh in range(2):
                                ins = e_.matmul(
                                    PS[tb * 2 + dh][:, 0:512],
                                    hTr[:, (fy % 8) * TCH + tb * 128:(fy % 8) * TCH + tb * 128 + 128],
                                    w2sb[:, fy * D + dh * 512:fy * D + dh * 512 + 512],
                                    start=(fy == 0), stop=(fy == NF - 1))
                        return ins
                    y_done[gfy] = sch.emit("tensor", ymm, waits=yw, inc=tT)
                sw = [(tT, h_done[gf])]
                if gf >= 8:
                    sw.append((tT, y_done[gf - 8]))
                hevac[gf] = sch.emit("scalar", lambda e_, f=f, tlen=tlen, bk=bk: e_.activation(
                    out=hTr[:, (f % 8) * TCH:(f % 8) * TCH + tlen],
                    in_=PS[bk][:, 0:tlen], func=AF.Relu),
                    waits=sw, inc=cS)
                bank_set(bk, cS, hevac[gf])
            fy = NF - 1
            gfy = gfbase + fy

            def ymmt(e_, fy=fy, ntb=ntb):
                ins = None
                for tb in range(ntb):
                    for dh in range(2):
                        ins = e_.matmul(
                            PS[tb * 2 + dh][:, 0:512],
                            hTr[:, (fy % 8) * TCH + tb * 128:(fy % 8) * TCH + tb * 128 + 128],
                            w2sb[:, fy * D + dh * 512:fy * D + dh * 512 + 512],
                            start=(fy == 0), stop=(fy == NF - 1))
                return ins
            y_done[gfy] = sch.emit("tensor", ymmt, waits=[(cS, hevac[gfy])], inc=tT)
            evw = [(tT, y_done[gfy])]
            if store_last is not None:
                evw.append((dG, store_last))
            ev = None
            for tb in range(ntb):
                gi = (c0 + tb * 128) // 128
                cbw = [comb_vals[gi]] if comb_vals[gi] else []
                ev = sch.emit("vector", lambda e_, tb=tb, gi=gi, e0=e: e_.tensor_scalar(
                    out=ysb[:, tb * D:tb * D + 512], in0=PS[tb * 2][:, 0:512],
                    scalar1=comb[:, gi * NE + e0:gi * NE + e0 + 1], scalar2=None, op0=ALU.mult),
                    waits=(evw + cbw if tb == 0 else cbw), inc=vV)
                ev = sch.emit("vector", lambda e_, tb=tb, gi=gi, e0=e: e_.tensor_scalar(
                    out=ysb[:, tb * D + 512:tb * D + 1024], in0=PS[tb * 2 + 1][:, 0:512],
                    scalar1=comb[:, gi * NE + e0:gi * NE + e0 + 1], scalar2=None, op0=ALU.mult),
                    inc=vV)
            yevac_last = ev
            for bk in range(2 * ntb):
                bank_set(bk, vV, ev)
            acc = {} if e == 0 else dict(accum_op=ALU.add)
            store_last = sch.emit("gpsimd", lambda e_, c0=c0, tlen=tlen, ntb=ntb, acc=acc: e_.dma_start(
                out=YS[c0:c0 + tlen, :].rearrange("(tb p) n -> p tb n", p=128),
                in_=ysb[:, 0:ntb * D].rearrange("p (tb n) -> p tb n", tb=ntb), **acc),
                waits=[(vV, ev)], inc=dG, by=16)
            if e == NE - 1:
                tiles = [(c0 + tb * 128) // 128 for tb in range(ntb)]
                for i_ in tiles:
                    ys_tile_vals[i_] = (dG, store_last)
                if out_cb:
                    out_cb(tiles)
        return ys_tile_vals

    def attention_prologue(sv):
        v_msk = sch.emit("sync", lambda e: e.dma_start(
            out=msk_sb[:, :].rearrange("p (k s) -> p k s", k=4),
            in_=msk_in.rearrange("k p s -> p k s")), inc=dS, by=16)
        v_lnw = sch.emit("sync", lambda e: e.dma_start(out=lnw_sb[0:1, :], in_=lnw_in), inc=dS, by=16)
        prev_g2 = None
        for h_ in range(NH):
            gw_ = [(dS, v_lnw)]
            if prev_g2 is not None:
                gw_.append((tT, prev_g2))
            g1 = sch.emit("scalar", lambda e, h=h_: e.add(
                grow[0:1, :], lnw_sb[0:1, h * DH:(h + 1) * DH], 1.0),
                waits=gw_, inc=cS)
            g2 = sch.emit("tensor", lambda e: e.matmul(
                PS[1][:, 0:DH], ones1[0:1, :], grow[0:1, :], start=True, stop=True),
                waits=[(cS, g1), bank_wait(1)], inc=tT)
            g3 = sch.emit("vector", lambda e, h=h_: e.tensor_copy(
                gamma[:, h * DH:(h + 1) * DH], PS[1][:, 0:DH]),
                waits=[(tT, g2)], inc=vV)
            bank_set(1, vV, g3)
            prev_g2 = g2
        return dict(sv, v_msk=v_msk)

    def attention_batch(b, sv, xn_vals, st, head_cb=None):
        att_vals = st["att_vals"]
        prev_head = st["prev_head"]
        prev_batch_om = st["prev_batch_om"]
        bS = b * S
        xh_val = {}

        def emit_xh(h, xhw):
            v = None
            for lc in range(2):
                v = sch.emit("sync", lambda e, lc=lc, h=h, bS=bS: e.dma_start_transpose(
                    out=xhT[:, lc * S:(lc + 1) * S],
                    in_=XN[bS:bS + S, (2 * h + lc) * 128:(2 * h + lc) * 128 + 128]),
                    waits=(xhw if lc == 0 else []), inc=dS, by=16)
            xh_val[h] = v
        zw = [(tT, prev_batch_om)] if prev_batch_om else []
        zv = sch.emit("vector", lambda e: e.memset(zer4[0:NH, :], 0.0), waits=zw, inc=vV)
        prev_gmm = None
        igev = fgev = None
        for g in range(8):
            lw = [(tT, prev_gmm)] if prev_gmm else ([(tT, sch.tT.v)] if b > 0 else [])
            lw.append(xn_vals[b * 32 + g * 4 + 3])
            xv = None
            for j in range(ND):
                xv = sch.emit("sync", lambda e, g=g, j=j, bS=bS: e.dma_start_transpose(
                    out=xg_sb[:, j * 512:(j + 1) * 512],
                    in_=XN[bS + g * 512:bS + (g + 1) * 512, j * 128:(j + 1) * 128]),
                    waits=(lw if j == 0 else []), inc=dS, by=16)
            mi = None
            for j in range(ND):
                mi = sch.emit("tensor", lambda e, j=j: e.matmul(
                    PS[0][0:NH, :], s3aT[:, j * NH:(j + 1) * NH],
                    xg_sb[:, j * 512:(j + 1) * 512],
                    start=(j == 0), stop=(j == ND - 1)),
                    waits=([(dS, xv), bank_wait(0)] if j == 0 else []),
                    inc=(tT if j == ND - 1 else None))
            mf = None
            for j in range(ND):
                mf = sch.emit("tensor", lambda e, j=j: e.matmul(
                    PS[1][0:NH, :], s3fT[:, j * NH:(j + 1) * NH],
                    xg_sb[:, j * 512:(j + 1) * 512],
                    start=(j == 0), stop=(j == ND - 1)),
                    waits=([bank_wait(1)] if j == 0 else []),
                    inc=(tT if j == ND - 1 else None))
            prev_gmm = mf
            igev = sch.emit("vector", lambda e, g=g: e.tensor_scalar(
                out=ig4[0:NH, g * 512:(g + 1) * 512], in0=PS[0][0:NH, :],
                scalar1=bi_sb[0:NH, 0:1], scalar2=None, op0=ALU.add),
                waits=[(tT, mi), (dS, sv["v_bi"])], inc=vV)
            bank_set(0, vV, igev)
            fgev = sch.emit("vector", lambda e, g=g: e.tensor_scalar(
                out=fg4[0:NH, g * 512:(g + 1) * 512], in0=PS[1][0:NH, :],
                scalar1=bf_sb[0:NH, 0:1], scalar2=None, op0=ALU.add),
                waits=[(tT, mf), (dS, sv["v_bf"])], inc=vV)
            bank_set(1, vV, fgev)
        xhw0 = [xn_vals[b * 32 + 31]]
        if prev_head is not None:
            xhw0.append((tT, prev_head["last_mm"]))
        emit_xh(0, xhw0)
        c1 = sch.emit("scalar", lambda e: e.activation(
            out=t4a[0:NH, :], in_=fg4[0:NH, :], func=AF.Abs),
            waits=[(vV, fgev)], inc=cS)
        c2 = sch.emit("scalar", lambda e: e.activation(
            out=t4a[0:NH, :], in_=t4a[0:NH, :], func=AF.Exp, scale=-1.0),
            waits=[(cS, c1)], inc=cS)
        c3 = sch.emit("scalar", lambda e: e.activation(
            out=t4a[0:NH, :], in_=t4a[0:NH, :], func=AF.Ln, bias=1.0),
            waits=[(cS, c2)], inc=cS)
        c4 = sch.emit("scalar", lambda e: e.activation(
            out=t4b[0:NH, :], in_=fg4[0:NH, :], func=AF.Relu, scale=-1.0),
            waits=[(cS, c3)], inc=cS)
        c5 = sch.emit("vector", lambda e: e.tensor_add(t4a[0:NH, :], t4a[0:NH, :], t4b[0:NH, :]),
                      waits=[(cS, c4)], inc=vV)
        c6 = sch.emit("vector", lambda e: e.tensor_scalar(
            out=t4a[0:NH, :], in0=t4a[0:NH, :], scalar1=-1.0, scalar2=None, op0=ALU.mult),
            waits=[(vV, c5)], inc=vV)
        c7 = sch.emit("vector", lambda e: e.tensor_tensor_scan(
            out=t4b[0:NH, :], data0=t4a[0:NH, :], data1=zer4[0:NH, :],
            initial=0.0, op0=ALU.add, op1=ALU.add),
            waits=[(vV, c6), (vV, zv)], inc=vV)
        c8 = sch.emit("vector", lambda e: e.tensor_sub(ig4[0:NH, :], ig4[0:NH, :], t4b[0:NH, :]),
                      waits=[(vV, c7), (vV, igev)], inc=vV)
        sa = sch.emit("sync", lambda e, b=b: e.dma_start(out=GS[b, 0], in_=ig4[0:NH, :]),
                      waits=[(vV, c8)], inc=dS, by=16)
        c9 = sch.emit("vector", lambda e: e.tensor_tensor_scan(
            out=t4a[0:NH, :], data0=ig4[0:NH, :], data1=ig4[0:NH, :],
            initial=-3.0e38, op0=ALU.max, op1=ALU.max),
            waits=[(vV, c8)], inc=vV)
        ca = sch.emit("vector", lambda e: e.tensor_scalar(
            out=fg4[0:NH, :], in0=t4a[0:NH, :], scalar1=-1.0, scalar2=None, op0=ALU.mult),
            waits=[(vV, c9)], inc=vV)
        sm = sch.emit("sync", lambda e, b=b: e.dma_start(out=GS[b, 1], in_=fg4[0:NH, :]),
                      waits=[(vV, ca)], inc=dS, by=16)
        cb = sch.emit("vector", lambda e: e.tensor_add(t4b[0:NH, :], t4b[0:NH, :], t4a[0:NH, :]),
                      waits=[(vV, c9)], inc=vV)
        cc = sch.emit("scalar", lambda e: e.activation(
            out=t4b[0:NH, :], in_=t4b[0:NH, :], func=AF.Exp, scale=-1.0),
            waits=[(vV, cb)], inc=cS)
        se = sch.emit("sync", lambda e, b=b: e.dma_start(out=GS[b, 2], in_=t4b[0:NH, :]),
                      waits=[(cS, cc)], inc=dS, by=16)

        for h in range(NH):
            raw = [(dS, sa), (vV, c9)]
            rmw = [(dS, sm)]
            rew = [(dS, se), (vV, cb)]
            if prev_head is not None:
                raw.append((tT, prev_head["cm"]))
                rmw.append((tT, prev_head["mbm"]))
                rew.append((tT, prev_head["cm"]))
            ra = sch.emit("sync", lambda e, b=b, h=h: e.dma_start(
                out=arow[0:1, :], in_=GS[b, 0, h].rearrange("(a s) -> a s", a=1)),
                waits=raw, inc=dS, by=16)
            rm = sch.emit("sync", lambda e, b=b, h=h: e.dma_start(
                out=mrow[0:1, :], in_=GS[b, 1, h].rearrange("(a s) -> a s", a=1)),
                waits=rmw, inc=dS, by=16)
            re_ = sch.emit("sync", lambda e, b=b, h=h: e.dma_start(
                out=erow[0:1, :], in_=GS[b, 2, h].rearrange("(a s) -> a s", a=1)),
                waits=rew, inc=dS, by=16)
            xh = xh_val[h]

            def colmm(e_):
                ins = None
                for i in range(32):
                    ins = e_.matmul(PS[6][:, i:i + 1], arow[0:1, i * 128:(i + 1) * 128],
                                    ones1[0:1, 0:1], start=True, stop=True)
                for i in range(32):
                    ins = e_.matmul(PS[6][:, 32 + i:33 + i], erow[0:1, i * 128:(i + 1) * 128],
                                    ones1[0:1, 0:1], start=True, stop=True)
                return ins
            cm = sch.emit("tensor", colmm,
                          waits=[(dS, ra), (dS, re_), bank_wait(6)], inc=tT)
            cev = sch.emit("vector", lambda e: e.tensor_copy(cols[:, :], PS[6][:, 0:64]),
                           waits=[(tT, cm)], inc=vV)
            bank_set(6, vV, cev)
            qk_rot = [2, 3, 6, 7]
            qe = ke = ve = None
            last_mm = None
            rot = 0
            for lc in range(2):
                j2 = 2 * h + lc
                for grp in range(8):
                    bk = qk_rot[rot % 4]; rot += 1
                    qv = sch.emit("tensor", lambda e, j2=j2, lc=lc, grp=grp, bk=bk: e.matmul(
                        PS[bk][:, :], wqkvb[:, (0 * 8 + j2) * 128:(0 * 8 + j2) * 128 + 128],
                        xhT[:, lc * S + grp * 512:lc * S + (grp + 1) * 512],
                        start=True, stop=True),
                        waits=[(dS, xh), (dG, sv["v_wqkv"]), bank_wait(bk)], inc=tT)
                    qe = sch.emit("scalar", lambda e, lc=lc, grp=grp, bk=bk: e.copy(
                        qT[:, lc * S + grp * 512:lc * S + (grp + 1) * 512], PS[bk][:, :]),
                        waits=[(tT, qv)], inc=cS)
                    bank_set(bk, cS, qe)
                    bk = qk_rot[rot % 4]; rot += 1
                    kv = sch.emit("tensor", lambda e, j2=j2, lc=lc, grp=grp, bk=bk: e.matmul(
                        PS[bk][:, :], wqkvb[:, (1 * 8 + j2) * 128:(1 * 8 + j2) * 128 + 128],
                        xhT[:, lc * S + grp * 512:lc * S + (grp + 1) * 512],
                        start=True, stop=True),
                        waits=[bank_wait(bk)], inc=tT)
                    ke = sch.emit("vector", lambda e, lc=lc, grp=grp, bk=bk: e.tensor_scalar(
                        out=kT[:, lc * S + grp * 512:lc * S + (grp + 1) * 512],
                        in0=PS[bk][:, :], scalar1=1.0 / 16.0, scalar2=None, op0=ALU.mult),
                        waits=[(tT, kv)], inc=vV)
                    bank_set(bk, vV, ke)
                for i0 in range(0, 32, 4):
                    bk = 4 + (i0 // 4) % 2
                    def vmm(e_, j2=j2, lc=lc, i0=i0, bk=bk):
                        ins = None
                        for q in range(4):
                            ins = e_.matmul(
                                PS[bk][:, q * 128:(q + 1) * 128],
                                xhT[:, lc * S + (i0 + q) * 128:lc * S + (i0 + q + 1) * 128],
                                wqkvb[:, (2 * 8 + j2) * 128:(2 * 8 + j2) * 128 + 128],
                                start=True, stop=True)
                        return ins
                    vv_ = sch.emit("tensor", vmm, waits=[bank_wait(bk)], inc=tT)
                    last_mm = vv_
                    ve = sch.emit("scalar", lambda e, lc=lc, i0=i0, bk=bk,
                                  _w=None: e.copy(
                        v_sb[:, :].rearrange("p (i c) -> p i c", c=264)[:, i0:i0 + 4, lc * 128:lc * 128 + 128],
                        PS[bk][:, :].rearrange("p (i c) -> p i c", c=128)),
                        waits=[(tT, vv_), (vV, c7)], inc=cS)
                    bank_set(bk, cS, ve)
            von = sch.emit("vector", lambda e: e.memset(
                v_sb[:, :].rearrange("p (i c) -> p i c", c=264)[:, :, 256:257], 1.0),
                waits=[(cS, ve)], inc=vV)
            if h + 1 < NH:
                emit_xh(h + 1, [(tT, last_mm)])

            mbm_last = None
            hstore = [None, None]
            last_dv = None
            last_gadd = None
            for sS_ in range(8):
                mbm = sch.emit("tensor", lambda e, sS_=sS_: e.matmul(
                    PS[7][:, :], ones1[0:1, :], mrow[0:1, sS_ * 512:(sS_ + 1) * 512],
                    start=True, stop=True),
                    waits=[(dS, rm), bank_wait(7)], inc=tT)
                mbm_last = mbm
                mbw = [(tT, mbm)]
                if last_dv is not None:
                    mbw.append((cS, last_dv))
                if last_gadd is not None:
                    mbw.append((gC, last_gadd))
                mb0 = sch.emit("vector", lambda e: e.tensor_copy(mb_sb[:, 0:512], PS[7][:, :]),
                               waits=mbw, inc=vV)
                mbv = mb0
                gadd = None
                for k in range(4):
                    gadd = sch.emit("gpsimd", lambda e, k=k: e.tensor_add(
                        mb_sb[:, (k + 1) * 512:(k + 2) * 512], mb_sb[:, 0:512],
                        msk_sb[:, k * 512:(k + 1) * 512]),
                        waits=[(dS, sv["v_msk"]), (vV, mb0)], inc=gC)
                last_gadd = gadd
                bank_set(7, vV, mb0)
                ntile = 4 * (sS_ + 1)
                prevC = {0: None, 1: None}
                prev_out = {0: None, 1: None}
                pend_out = None

                def emit_out(ti, par, cv):
                    ow = [(vV, cv), (vV, von)]
                    if ti == 0:
                        for bk in range(4):
                            ow.append(bank_wait(bk))
                        ow.append((cS, ve))

                    def outmm(e_, ti=ti, par=par, last=(ti == ntile - 1)):
                        ins = None
                        for sb in range(4):
                            ins = e_.matmul(
                                PS[sb][:, 0:257], cT_sb[par][:, sb * 128:(sb + 1) * 128],
                                v_sb[:, ti * 264:ti * 264 + 257],
                                start=(ti == 0), stop=last)
                        return ins
                    return sch.emit("tensor", outmm, waits=ow, inc=tT)

                cv_of = {}
                om = None
                for ti in range(ntile):
                    par = ti % 2
                    qbk = 4 + par

                    def qkmm(e_, ti=ti, sS_=sS_, qbk=qbk):
                        ins = None
                        for lc in range(2):
                            ins = e_.matmul(
                                PS[qbk][:, :],
                                kT[:, lc * S + ti * 128:lc * S + (ti + 1) * 128],
                                qT[:, lc * S + sS_ * 512:lc * S + (sS_ + 1) * 512],
                                start=(lc == 0), stop=(lc == 1))
                        return ins
                    qw = [bank_wait(qbk)]
                    if sS_ == 0 and ti < 2:
                        qw += [(vV, ke), (cS, qe)]
                    qkv_ = sch.emit("tensor", qkmm, waits=qw, inc=tT)
                    if pend_out is not None:
                        pti, ppar = pend_out
                        om = emit_out(pti, ppar, cv_of[pti])
                        prev_out[ppar] = om
                    variant = 0 if ti < 4 * sS_ else 1 + (ti - 4 * sS_)
                    dwaits = [(vV, mb0), (vV, cev)]
                    if variant > 0:
                        dwaits.append((gC, gadd))
                    if prevC[par] is not None:
                        dwaits.append((vV, prevC[par]))
                    dv = sch.emit("scalar", lambda e, par=par, ti=ti, variant=variant: e.activation(
                        out=D_sb[par][:, :], in_=mb_sb[:, variant * 512:(variant + 1) * 512],
                        func=AF.Exp, bias=cols[:, ti:ti + 1]),
                        waits=dwaits, inc=cS)
                    last_dv = dv
                    cwaits = [(tT, qkv_), (cS, dv)]
                    if prev_out[par] is not None:
                        cwaits.append((tT, prev_out[par]))
                    cv = sch.emit("vector", lambda e, par=par, qbk=qbk: e.tensor_tensor(
                        cT_sb[par][:, :], PS[qbk][:, :], D_sb[par][:, :], ALU.mult),
                        waits=cwaits, inc=vV)
                    prevC[par] = cv
                    cv_of[ti] = cv
                    bank_set(qbk, vV, cv)
                    pend_out = (ti, par)
                pti, ppar = pend_out
                om = emit_out(pti, ppar, cv_of[pti])
                prev_out[ppar] = om
                for sb in range(4):
                    srow = bS + sS_ * 512 + sb * 128
                    gi = sS_ * 4 + sb
                    a0 = sb * 12
                    nb_ = no_sb[sb]
                    p1 = sch.emit("scalar", lambda e, sb=sb, a0=a0: e.activation(
                        out=ast[:, a0:a0 + 1], in_=PS[sb][:, 256:257], func=AF.Abs),
                        waits=[(tT, om)], inc=cS)
                    p2 = sch.emit("vector", lambda e, gi=gi, a0=a0: e.tensor_tensor(
                        ast[:, a0 + 1:a0 + 2], ast[:, a0:a0 + 1], cols[:, 32 + gi:33 + gi],
                        ALU.max), waits=[(cS, p1), (tT, om)], inc=vV)
                    p3 = sch.emit("vector", lambda e, a0=a0: e.tensor_scalar(
                        out=ast[:, a0 + 1:a0 + 2], in0=ast[:, a0 + 1:a0 + 2], scalar1=1e-8,
                        scalar2=None, op0=ALU.add), waits=[(vV, p2)], inc=vV)
                    p4 = sch.emit("vector", lambda e, a0=a0: e.reciprocal(
                        ast[:, a0 + 2:a0 + 3], ast[:, a0 + 1:a0 + 2]),
                        waits=[(vV, p3)], inc=vV)
                    p5 = sch.emit("vector", lambda e, sb=sb, a0=a0, nb_=nb_: e.tensor_scalar(
                        out=nb_[:, :], in0=PS[sb][:, 0:256], scalar1=ast[:, a0 + 2:a0 + 3],
                        scalar2=0.0, op0=ALU.mult, op1=ALU.add, accum_out=ast[:, a0 + 3:a0 + 4]),
                        waits=[(vV, p4)], inc=vV)
                    bank_set(sb, vV, p5)
                    p6 = sch.emit("scalar", lambda e, nb_=nb_, a0=a0: e.activation(
                        out=sq2[:, :], in_=nb_[:, :], func=AF.Square, accum_out=ast[:, a0 + 4:a0 + 5]),
                        waits=[(vV, p5)], inc=cS)
                    p7 = sch.emit("vector", lambda e, a0=a0: e.tensor_scalar(
                        out=ast[:, a0 + 5:a0 + 6], in0=ast[:, a0 + 3:a0 + 4], scalar1=1.0 / 256,
                        scalar2=None, op0=ALU.mult), waits=[(vV, p5)], inc=vV)
                    p8 = sch.emit("vector", lambda e, a0=a0: e.tensor_scalar(
                        out=ast[:, a0 + 6:a0 + 7], in0=ast[:, a0 + 4:a0 + 5], scalar1=1.0 / 256,
                        scalar2=None, op0=ALU.mult), waits=[(cS, p6)], inc=vV)
                    p9 = sch.emit("vector", lambda e, a0=a0: e.tensor_mul(
                        ast[:, a0 + 7:a0 + 8], ast[:, a0 + 5:a0 + 6], ast[:, a0 + 5:a0 + 6]),
                        waits=[(vV, p7)], inc=vV)
                    pa = sch.emit("vector", lambda e, a0=a0: e.tensor_sub(
                        ast[:, a0 + 8:a0 + 9], ast[:, a0 + 6:a0 + 7], ast[:, a0 + 7:a0 + 8]),
                        waits=[(vV, p8), (vV, p9)], inc=vV)
                    pb = sch.emit("scalar", lambda e, a0=a0: e.activation(
                        out=ast[:, a0 + 9:a0 + 10], in_=ast[:, a0 + 8:a0 + 9], func=AF.Ln,
                        bias=epsc[:, 0:1]), waits=[(vV, pa)], inc=cS)
                    pc = sch.emit("scalar", lambda e, a0=a0: e.activation(
                        out=ast[:, a0 + 10:a0 + 11], in_=ast[:, a0 + 9:a0 + 10], func=AF.Exp,
                        scale=-0.5), waits=[(cS, pb)], inc=cS)
                    pd = sch.emit("vector", lambda e, nb_=nb_, a0=a0: e.tensor_scalar(
                        out=nb_[:, :], in0=nb_[:, :], scalar1=ast[:, a0 + 5:a0 + 6],
                        scalar2=ast[:, a0 + 10:a0 + 11], op0=ALU.subtract, op1=ALU.mult),
                        waits=[(cS, pc)], inc=vV)
                    hpar = gi % 2
                    pew = [(vV, pd)]
                    if hstore[hpar] is not None:
                        pew.append((dS, hstore[hpar]))
                    pe_ = sch.emit("vector", lambda e, h=h, hpar=hpar, nb_=nb_: e.tensor_mul(
                        hout[hpar][:, :], nb_[:, :], gamma[:, h * DH:(h + 1) * DH]),
                        waits=pew, inc=vV)
                    hstore[hpar] = sch.emit("sync", lambda e, srow=srow, h=h, hpar=hpar: e.dma_start(
                        out=ATT[srow:srow + 128, h * DH:(h + 1) * DH], in_=hout[hpar][:, :]),
                        waits=[(vV, pe_)], inc=dS, by=16)
                    if h == NH - 1:
                        att_vals[srow // 128] = (dS, hstore[hpar])
            prev_head = dict(cm=cm, mbm=mbm_last, last_mm=last_mm)
            prev_batch_om = om
            if head_cb is not None:
                head_cb(h)
        st["prev_head"] = prev_head
        st["prev_batch_om"] = prev_batch_om

    def snap_all():
        return [(x, x.v) for x in sch.sems]

    def set_pend(snap):
        for e in ENGS:
            sch.pend[e] = sch.pend[e] + list(snap)

    sv = setup()
    xnv0 = [None] * NTT
    cmb0 = [None] * NTT
    xnv2 = [None] * NTT
    x2v = [None] * NTT
    ysv1_arr = [None] * NTT
    ln0 = ln_maker(x_in, None, None, XN, xnv0, [None] * NTT)
    ln2 = ln_maker(x_in, YS, X2, XN, xnv2, x2v, src1_vals=ysv1_arr)
    r0st = {}
    cnt0 = [0, 0]

    def in_cb1(hwm):
        gneed = hwm // 4
        tneed = max(hwm, 4 * gneed + 3)
        while cnt0[0] <= tneed:
            ln0(cnt0[0])
            cnt0[0] += 1
        while cnt0[1] <= gneed:
            router_g(sv, cnt0[1], xnv0, cmb0, r0st)
            cnt0[1] += 1

    def out_cb1(tiles):
        for i in tiles:
            ln2(i)

    sch.phase = "P1"
    ffn_call(xnv0, cmb0, ysv1_arr, w0_waits=[(tT, sv["t_setup"])],
             in_cb=in_cb1, out_cb=out_cb1)
    snapP1 = snap_all()
    set_pend(snapP1)
    sch.phase = "P3pro"
    sv = attention_prologue(sv)
    st = dict(att_vals=[None] * NTT, prev_head=None, prev_batch_om=None)
    attv = st["att_vals"]
    xnv4 = [None] * NTT
    x3v = [None] * NTT
    cmb4 = [None] * NTT
    ysv5_arr = [None] * NTT
    ln4 = ln_maker(X2, ATT, X3, XN, xnv4, x3v, src0_vals=x2v, src1_vals=attv)
    ln6 = ln_maker(X3, YS, out_hb, None, [None] * NTT, [None] * NTT,
                   src0_vals=x3v, src1_vals=ysv5_arr)
    r4st = {}
    cnt4 = [0, 0]

    def ensure_p4(tneed, gneed):
        while cnt4[0] <= tneed:
            ln4(cnt4[0])
            cnt4[0] += 1
        while cnt4[1] <= gneed:
            router_g(sv, cnt4[1], xnv4, cmb4, r4st)
            cnt4[1] += 1

    def head_cb(h):
        ensure_p4(8 * h + 7, 2 * h - 1)

    sch.phase = "P3b0"
    attention_batch(0, sv, xnv2, st)
    sch.phase = "P3b1"
    attention_batch(1, sv, xnv2, st, head_cb=head_cb)
    snapP3 = snap_all()
    set_pend(snapP3)
    sch.phase = "P5pre"
    pre5 = ffn_pre(xnv4)

    def in_cb5(hwm):
        gneed = hwm // 4
        ensure_p4(max(hwm, 4 * gneed + 3), gneed)

    def out_cb5(tiles):
        for i in tiles:
            ln6(i)

    sch.phase = "P5"
    ffn_call(xnv4, cmb4, ysv5_arr, pre=pre5, in_cb=in_cb5, out_cb=out_cb5)
    sch.emit("sync", lambda e: e.nop(),
             waits=[(sch.dS, sch.dS.v), (sch.dG, sch.dG.v), (sch.dV, sch.dV.v), (sch.dC, sch.dC.v)])

    with nc.Block() as block:
        @block.sync
        def _(e):
            sch.run("sync", e)

        @block.gpsimd
        def _(e):
            sch.run("gpsimd", e)

        @block.tensor
        def _(e):
            sch.run("tensor", e)

        @block.vector
        def _(e):
            sch.run("vector", e)

        @block.scalar
        def _(e):
            sch.run("scalar", e)

    for s in sch.sems:
        s.cm.__exit__(None, None, None)
    return nc


def _blockdiag_blocks(w):
    out = np.zeros((8, 128, 128), dtype=np.float32)
    for j in range(8):
        for g in range(32):
            out[j, 4 * g:4 * g + 4, 4 * g:4 * g + 4] = w[32 * j + g].T
    return out


def kernel(L0_attn_x, L0_attn_wq, L0_attn_wk, L0_attn_wv, L0_attn_wi, L0_attn_bi,
           L0_attn_wf, L0_attn_bf, L0_attn_ln_w, L0_ffn_gate_w, L0_ffn_w1, L0_ffn_w2):
    x = np.ascontiguousarray(np.asarray(L0_attn_x, dtype=np.float32))
    wqkv = np.concatenate([_blockdiag_blocks(np.asarray(w, dtype=np.float32))
                           for w in (L0_attn_wq, L0_attn_wk, L0_attn_wv)], axis=0)
    msk = np.zeros((4, 128, 512), dtype=np.float32)
    for k in range(4):
        ti = np.arange(128)[:, None] + k * 128
        sj = np.arange(512)[None, :]
        msk[k] = np.where(ti <= sj, 0.0, -1e30)
    shared = {
        "w1": np.ascontiguousarray(np.asarray(L0_ffn_w1, dtype=np.float32)),
        "w2": np.ascontiguousarray(np.asarray(L0_ffn_w2, dtype=np.float32)),
        "gw": np.ascontiguousarray(np.asarray(L0_ffn_gate_w, dtype=np.float32)),
        "wqkv": wqkv,
        "wi": np.ascontiguousarray(np.asarray(L0_attn_wi, dtype=np.float32)),
        "wf": np.ascontiguousarray(np.asarray(L0_attn_wf, dtype=np.float32)),
        "bi": np.asarray(L0_attn_bi, dtype=np.float32).reshape(NH, 1),
        "bfg": np.asarray(L0_attn_bf, dtype=np.float32).reshape(NH, 1),
        "lnw": np.asarray(L0_attn_ln_w, dtype=np.float32).reshape(1, D),
        "msk": msk,
        "idn": np.eye(128, dtype=np.float32),
    }
    nc = build_core_program()
    in_maps = []
    for c in range(8):
        m = dict(shared)
        m["x"] = np.ascontiguousarray(x[2 * c:2 * c + 2].reshape(T, D))
        in_maps.append(m)
    res = run_bass_kernel_spmd(nc, in_maps, core_ids=list(range(8)))
    out = np.concatenate([res.results[c]["out"].reshape(2, S, D) for c in range(8)], axis=0)
    return out
